# version 09
# baseline (speedup 1.0000x reference)
import numpy as np
import ml_dtypes
from contextlib import ExitStack

import concourse.bass as bass
import concourse.mybir as mybir
from concourse.bass_utils import run_bass_kernel_spmd

F32 = mybir.dt.float32
BF16 = mybir.dt.bfloat16
AF = mybir.ActivationFunctionType
ALU = mybir.AluOpType
AX = mybir.AxisListType
BFNP = ml_dtypes.bfloat16

NEG = -1e30
LN_EPS = 1e-5
RMS_EPS = 1e-6
THETA = 10000.0


class Cfg:
    def __init__(self, D, NH, T_OWN, W, FS, F, E=8):
        self.D, self.NH, self.T_OWN, self.W, self.FS, self.F, self.E = D, NH, T_OWN, W, FS, F, E
        assert NH * 128 == D
        assert T_OWN % 512 == 0 and W % 512 == 0 and D % 128 == 0
        self.T_EXT = W
        self.T_FRAME = W + T_OWN
        self.DC = D // 128
        self.NTT = self.T_FRAME // 128
        self.NTT_OWN = T_OWN // 128
        self.NCH = self.T_FRAME // 512
        self.WT = W // 128
        self.MT = self.WT + 4
        self.QC = T_OWN // 512
        self.NFS = FS // 128
        self.NF = F // 128
        self.CGW = min(512, D)
        self.NCG = D // self.CGW
        self.HCG = self.CGW // 128
        self.OCW = min(512, D)
        self.NOC = D // self.OCW
        self.HALF = T_OWN // 2
        assert self.HALF % 512 == 0
        self.NT5 = self.HALF // 512
        self.NTH = self.HALF // 128
        self.CAP = (3 * T_OWN) // 8
        assert self.CAP % 128 == 0
        self.NSUB = self.CAP // 128


FULL = Cfg(D=2048, NH=16, T_OWN=2048, W=4096, FS=2816, F=1408)


class Chain:
    def __init__(self, nc, ctx, name):
        self.h = ctx.enter_context(nc.semaphore(name))
        self.n = 0

    def inc(self, instr, k=1):
        instr.then_inc(self.h, k)
        self.n += k
        return self.n

    def wait(self, eng, n=None):
        eng.wait_ge(self.h, self.n if n is None else n)


def _build(cfg):
    c = cfg
    nc = bass.Bass("TRN2", target_bir_lowering=False)

    x_pad = nc.dram_tensor("x_pad", [c.T_FRAME, c.D], F32, kind="ExternalInput").ap()
    wqkv = nc.dram_tensor("wqkv", [c.D, 3 * c.D], F32, kind="ExternalInput").ap()
    wout = nc.dram_tensor("wout", [c.D, c.D], F32, kind="ExternalInput").ap()
    wr_t = nc.dram_tensor("wr_t", [c.D, c.E], F32, kind="ExternalInput").ap()
    wsg_t = nc.dram_tensor("wsg_t", [c.D, c.FS], F32, kind="ExternalInput").ap()
    wsu_t = nc.dram_tensor("wsu_t", [c.D, c.FS], F32, kind="ExternalInput").ap()
    wsd_t = nc.dram_tensor("wsd_t", [c.FS, c.D], F32, kind="ExternalInput").ap()
    wg_t = nc.dram_tensor("wg_t", [c.E, c.D, c.F], F32, kind="ExternalInput").ap()
    wu_t = nc.dram_tensor("wu_t", [c.E, c.D, c.F], F32, kind="ExternalInput").ap()
    wd_t = nc.dram_tensor("wd_t", [c.E, c.F, c.D], F32, kind="ExternalInput").ap()
    gq_row = nc.dram_tensor("gq_row", [1, 128], F32, kind="ExternalInput").ap()
    gk_row = nc.dram_tensor("gk_row", [1, 128], F32, kind="ExternalInput").ap()
    cos_t = nc.dram_tensor("cos_t", [c.T_FRAME, 128], F32, kind="ExternalInput").ap()
    sin_t = nc.dram_tensor("sin_t", [c.T_FRAME, 128], F32, kind="ExternalInput").ap()
    maskd = nc.dram_tensor("maskd", [128, 128], BF16, kind="ExternalInput").ap()
    maskw = nc.dram_tensor("maskw", [128, 128], BF16, kind="ExternalInput").ap()
    ident = nc.dram_tensor("ident", [128, 128], BF16, kind="ExternalInput").ap()
    validc1 = nc.dram_tensor("validc1", [128, c.NTT], F32, kind="ExternalInput").ap()
    triu = nc.dram_tensor("triu", [128, 128], BF16, kind="ExternalInput").ap()
    iota_h = nc.dram_tensor("iota_h", [128, c.NTT_OWN], mybir.dt.int32,
                            kind="ExternalInput").ap()

    out = nc.dram_tensor("out", [c.T_OWN, c.D], F32, kind="ExternalOutput").ap()

    xn1_d = nc.dram_tensor("xn1_d", [c.DC, 128, c.T_FRAME], BF16).ap()
    q_d = nc.dram_tensor("q_d", [c.NH, 128, c.T_OWN], BF16).ap()
    k_d = nc.dram_tensor("k_d", [c.NH, 128, c.T_FRAME], BF16).ap()
    v_d = nc.dram_tensor("v_d", [c.NTT, 128, c.NH * 129], BF16).ap()
    xp_d = nc.dram_tensor("xp_d", [c.T_OWN, c.D], F32).ap()
    xn2_d = nc.dram_tensor("xn2_d", [c.DC, 128, c.T_OWN], BF16).ap()
    xn2t_d = nc.dram_tensor("xn2t_d", [c.T_OWN, c.D], BF16).ap()
    wf_d = nc.dram_tensor("wf_d", [c.T_OWN, c.E], F32).ap()
    re_d = [nc.dram_tensor(f"re_d{i}", [c.T_OWN, c.D], F32).ap() for i in range(2)]
    idx_d = [nc.dram_tensor(f"idx_d{i}", [128, c.NSUB], mybir.dt.int32).ap()
             for i in range(2)]

    TEN, VEC, SCA, SYN, GPS = nc.tensor, nc.vector, nc.scalar, nc.sync, nc.gpsimd

    top = ExitStack()
    _uid = [0]

    def _sb(st, name, shape, dt):
        _uid[0] += 1
        return st.enter_context(nc.sbuf_tensor(f"{name}_{_uid[0]}", shape, dt))

    def _pt(st, name, shape):
        _uid[0] += 1
        return st.enter_context(nc.psum_tensor(f"{name}_{_uid[0]}", shape, F32))

    with top:
        c_mm = Chain(nc, top, "c_mm")
        c_ve = Chain(nc, top, "c_ve")
        c_se = Chain(nc, top, "c_se")
        c_gp = Chain(nc, top, "c_gp")
        s_sy = Chain(nc, top, "s_sy")
        s_sc = Chain(nc, top, "s_sc")
        s_gl = Chain(nc, top, "s_gl")
        s_out = Chain(nc, top, "s_out")
        sp = [Chain(nc, top, f"s_p{i}") for i in range(2)]
        sq = [Chain(nc, top, f"s_q{i}") for i in range(2)]
        sr = [Chain(nc, top, f"s_r{i}") for i in range(2)]

        ALL_ENG = [TEN, VEC, SCA, SYN, GPS]
        ALL_CH = [c_mm, c_ve, c_se, c_gp, s_sy, s_sc, s_gl] + sp + sq + sr

        def barrier():
            for eng in ALL_ENG:
                for ch in ALL_CH:
                    if ch.n:
                        ch.wait(eng)

        def vsync():
            VEC.wait_ge(c_ve.h, c_ve.n)

        def ssync():
            SCA.wait_ge(c_se.h, c_se.n)

        def gsync():
            GPS.wait_ge(c_gp.h, c_gp.n)

        def emit_ln(x_ap, xsq_ap, lnst, out_ap):
            c_se.inc(SCA.activation(xsq_ap, x_ap, AF.Square,
                                    accum_out=lnst[:, 0:1]))
            c_ve.inc(VEC.reduce_sum(lnst[:, 1:2], x_ap, axis=AX.X))
            c_se.wait(VEC)
            vsync()
            c_ve.inc(VEC.tensor_scalar_mul(lnst[:, 2:3], lnst[:, 1:2], 1.0 / c.D))
            c_ve.inc(VEC.tensor_scalar_mul(lnst[:, 3:4], lnst[:, 0:1], 1.0 / c.D))
            vsync()
            c_ve.inc(VEC.scalar_tensor_tensor(
                lnst[:, 4:5], lnst[:, 2:3], lnst[:, 2:3], lnst[:, 3:4],
                ALU.mult, ALU.subtract))
            c_ve.wait(SCA)
            c_se.inc(SCA.activation(lnst[:, 5:6], lnst[:, 4:5], AF.Sqrt,
                                    bias=K_LNEPS, scale=K_M1))
            c_se.wait(VEC)
            c_ve.inc(VEC.reciprocal(lnst[:, 6:7], lnst[:, 5:6]))
            vsync()
            c_ve.inc(VEC.scalar_tensor_tensor(
                lnst[:, 7:8], lnst[:, 2:3], -1.0, lnst[:, 6:7],
                ALU.mult, ALU.mult))
            vsync()
            c_ve.inc(VEC.tensor_scalar(
                out_ap, x_ap, lnst[:, 6:7], lnst[:, 7:8], ALU.mult, ALU.add))

        maskd_sb = _sb(top, "maskd_sb", [128, 128], BF16)
        maskw_sb = _sb(top, "maskw_sb", [128, 128], BF16)
        ident_sb = _sb(top, "ident_sb", [128, 128], BF16)
        validc1_sb = _sb(top, "validc1_sb", [128, c.NTT], F32)
        gq_sb = _sb(top, "gq_sb", [1, 128], F32)
        gk_sb = _sb(top, "gk_sb", [1, 128], F32)

        consts = _sb(top, "consts", [128, 8], F32)
        CONST_VALS = [LN_EPS, -1.0, 128.0 * RMS_EPS, RMS_EPS, 1.0 / 128.0, 1.0, 0.0, 0.0]
        for i, v in enumerate(CONST_VALS):
            c_ve.inc(VEC.memset(consts[:, i:i + 1], v))
        K_LNEPS = consts[:, 0:1]
        K_M1 = consts[:, 1:2]
        K_QEPS = consts[:, 2:3]
        K_KEPS = consts[:, 3:4]
        K_I128 = consts[:, 4:5]

        s_sy.inc(SYN.dma_start(maskd_sb[:], maskd[:, :]), 16)
        s_sy.inc(SYN.dma_start(maskw_sb[:], maskw[:, :]), 16)
        s_sy.inc(SYN.dma_start(ident_sb[:], ident[:, :]), 16)
        s_sy.inc(SYN.dma_start(validc1_sb[:], validc1[:, :]), 16)
        s_sy.inc(SYN.dma_start(gq_sb[:], gq_row[:, :]), 16)
        s_sy.inc(SYN.dma_start(gk_sb[:], gk_row[:, :]), 16)

        def proj_pass(kind):
            norm = kind in ("k", "q")
            colbase = {"q": 0, "k": c.D, "v": 2 * c.D}[kind]
            chunks = list(range(c.NCH)) if kind in ("k", "v") else \
                list(range(c.T_EXT // 512, c.NCH))
            with ExitStack() as ps:
                w_sb = _sb(ps, "w_sb", [128, c.DC, c.D], BF16)
                xn1fm = [_sb(ps, f"xn1fm{i}", [128, c.DC, 512], BF16) for i in range(2)]
                NBK = 6
                pp = [_pt(ps, f"pp{i}", [128, 512])
                      for i in range(NBK)]
                bank_ve = [0] * NBK
                bank_se = [0] * NBK
                for cc in range(c.DC):
                    s_gl.inc(GPS.dma_start(
                        w_sb[:, cc, :],
                        wqkv[cc * 128:(cc + 1) * 128, colbase:colbase + c.D]), 16)
                wres_n = s_gl.n

                if norm:
                    g_sb = gq_sb if kind == "q" else gk_sb
                    gb = _sb(ps, "gb", [128, 128], F32)
                    pbc = _pt(ps, "pbc", [128, 128])
                    onef = _sb(ps, "onef", [1, 128], F32)
                    VEC.wait_ge(s_sy.h, s_sy.n)
                    c_ve.inc(VEC.memset(onef[:], 1.0))
                    c_ve.wait(TEN)
                    TEN.wait_ge(s_sy.h, s_sy.n)
                    c_mm.inc(TEN.matmul(pbc[:], onef[:], g_sb[:], start=True, stop=True))
                    c_mm.wait(VEC)
                    c_ve.inc(VEC.tensor_copy(gb[:], pbc[:]))
                    tabraw = [_sb(ps, f"tr{i}", [128, 2, 4, 128], F32) for i in range(2)]
                    csg = [_sb(ps, f"cs{i}", [128, 2, 4, 128], BF16) for i in range(2)]
                    kcp = [_sb(ps, f"kcp{i}", [128, 512], BF16) for i in range(NBK)]
                    t1b = [_sb(ps, f"t1b{i}", [128, 512], BF16) for i in range(2)]
                    t2b = [_sb(ps, f"t2b{i}", [128, 512], BF16) for i in range(2)]
                    t12b = [_sb(ps, f"t12b{i}", [128, 512], BF16) for i in range(2)]
                    sqb = [_sb(ps, f"sqb{i}", [128, 512], BF16) for i in range(NBK)]
                    ssq = _sb(ps, "ssq", [128, c.NH], F32)
                    s4 = [_sb(ps, f"s4_{i}", [128, c.HCG], F32) for i in range(2)]
                    s4t = [_sb(ps, f"s4t{i}", [128, c.HCG], F32) for i in range(2)]
                    otile = [_sb(ps, f"ot{i}", [128, c.D], BF16) for i in range(2)]
                    fmst = [_sb(ps, f"fm{i}", [128, c.NH, 128], BF16) for i in range(2)]
                else:
                    vt = [_sb(ps, f"vt{i}", [128, c.NH, 129], BF16) for i in range(2)]

                if kind == "k":
                    x_f = [_sb(ps, f"x_f{i}", [128, c.D], F32) for i in range(2)]
                    xsq = [_sb(ps, f"xsq{i}", [128, c.D], BF16) for i in range(2)]
                    xn1t = [_sb(ps, f"xn1t{i}", [128, c.D], BF16) for i in range(2)]
                    lnst = _sb(ps, "lnst", [128, 8], F32)
                    xf_free = [(0, 0), (0, 0)]
                    xt_free = [0, 0]

                fm_cons = [0, 0]
                fmW_free = [0, 0]
                ot_free = [0, 0]
                fmst_free = [0, 0]
                tab_cons = [0, 0]
                tabraw_free = [0, 0]

                for chi, ch in enumerate(chunks):
                    par = chi % 2
                    if kind == "k":
                        for tt in range(4):
                            tp = tt % 2
                            ft = ch * 4 + tt
                            se0, ve0 = xf_free[tp]
                            SYN.wait_ge(c_se.h, se0)
                            SYN.wait_ge(c_ve.h, ve0)
                            ld = sp[tp]
                            ld.inc(SYN.dma_start(
                                x_f[tp][:], x_pad[ft * 128:(ft + 1) * 128, :]), 16)
                            SCA.wait_ge(ld.h, ld.n)
                            VEC.wait_ge(ld.h, ld.n)
                            VEC.wait_ge(s_sc.h, xt_free[tp])
                            emit_ln(x_f[tp][:], xsq[tp][:], lnst, xn1t[tp][:])
                            xf_free[tp] = (c_se.n, c_ve.n)
                            SCA.wait_ge(c_ve.h, c_ve.n)
                            if tt == 0:
                                SCA.wait_ge(c_mm.h, fm_cons[par])
                                SCA.wait_ge(s_sc.h, fmW_free[par])
                            s_sc.inc(SCA.dma_start(
                                xn1fm[par][:, :, tt * 128:(tt + 1) * 128],
                                xn1t[tp][:].rearrange("p (b d) -> p b d", d=128),
                                transpose=True), 16)
                            xt_free[tp] = s_sc.n
                        SCA.wait_ge(s_sc.h, s_sc.n)
                        s_sc.inc(SCA.dma_start(
                            xn1_d[:, :, ch * 512:(ch + 1) * 512].transpose([1, 0, 2]),
                            xn1fm[par][:, :, :]), 16)
                        fmW_free[par] = s_sc.n
                        fm_mode, fm_val = "sc", s_sc.n
                    else:
                        SYN.wait_ge(c_mm.h, fm_cons[par])
                        ld = sq[par]
                        ld.inc(SYN.dma_start(
                            xn1fm[par][:, :, :],
                            xn1_d[:, :, ch * 512:(ch + 1) * 512].transpose([1, 0, 2])), 16)
                        fm_mode, fm_val = "ld", (ld, ld.n)

                    if norm:
                        ldt = sr[par]
                        SYN.wait_ge(c_ve.h, tabraw_free[par])
                        ldt.inc(SYN.dma_start(
                            tabraw[par][:, 0],
                            cos_t[ch * 512:(ch + 1) * 512, :].rearrange(
                                "(a p) d -> p a d", p=128)), 16)
                        ldt.inc(SYN.dma_start(
                            tabraw[par][:, 1],
                            sin_t[ch * 512:(ch + 1) * 512, :].rearrange(
                                "(a p) d -> p a d", p=128)), 16)
                        VEC.wait_ge(ldt.h, ldt.n)
                        gbb = gb[:].unsqueeze(1).broadcast_to([128, 4, 128])
                        c_ve.inc(VEC.tensor_tensor(csg[par][:, 0],
                                                   tabraw[par][:, 0], gbb, ALU.mult))
                        c_ve.inc(VEC.tensor_tensor(
                            csg[par][:, 1, :, 0:64], tabraw[par][:, 1, :, 0:64],
                            gb[:, 64:128].unsqueeze(1).broadcast_to([128, 4, 64]),
                            ALU.mult))
                        c_ve.inc(VEC.tensor_tensor(
                            csg[par][:, 1, :, 64:128], tabraw[par][:, 1, :, 64:128],
                            gb[:, 0:64].unsqueeze(1).broadcast_to([128, 4, 64]),
                            ALU.mult))
                        tab_cons[par] = c_ve.n
                        tabraw_free[par] = c_ve.n

                    for tt in range(4):
                        ft = ch * 4 + tt
                        tp = tt % 2
                        for cg in range(c.NCG):
                            gi = (chi * 4 + tt) * c.NCG + cg
                            bk = gi % NBK
                            b2 = gi % 2
                            ps_t = pp[bk]
                            if fm_mode == "sc":
                                TEN.wait_ge(s_sc.h, fm_val)
                            else:
                                TEN.wait_ge(fm_val[0].h, fm_val[1])
                            TEN.wait_ge(s_gl.h, wres_n)
                            TEN.wait_ge(c_se.h, bank_se[bk])
                            for cc in range(c.DC):
                                ins = TEN.matmul(
                                    ps_t[:, 0:c.CGW],
                                    xn1fm[par][:, cc, tt * 128:(tt + 1) * 128],
                                    w_sb[:, cc, cg * c.CGW:(cg + 1) * c.CGW],
                                    start=(cc == 0), stop=(cc == c.DC - 1))
                            c_mm.inc(ins)
                            fm_cons[par] = c_mm.n
                            ps3 = ps_t[:, 0:c.CGW].rearrange("p (h d) -> p h d", d=128)
                            if norm:
                                c_mm.wait(SCA)
                                SCA.wait_ge(c_ve.h, bank_ve[bk])
                                c_se.inc(SCA.copy(kcp[bk][:, 0:c.CGW], ps_t[:, 0:c.CGW]))
                                c_se.inc(SCA.activation(
                                    sqb[bk][:, 0:c.CGW], ps_t[:, 0:c.CGW], AF.Square))
                                bank_se[bk] = c_se.n
                                c_se.wait(VEC)
                                c_ve.inc(VEC.reduce_sum(
                                    ssq[:, cg * c.HCG:(cg + 1) * c.HCG],
                                    sqb[bk][:, 0:c.CGW].rearrange("p (h d) -> p h d",
                                                                  d=128),
                                    axis=AX.X))
                                vsync()
                                if kind == "q":
                                    c_ve.wait(SCA)
                                    c_se.inc(SCA.activation(
                                        s4t[b2][:], ssq[:, cg * c.HCG:(cg + 1) * c.HCG],
                                        AF.Sqrt, bias=K_QEPS))
                                else:
                                    c_ve.wait(SCA)
                                    c_se.inc(SCA.activation(
                                        s4t[b2][:], ssq[:, cg * c.HCG:(cg + 1) * c.HCG],
                                        AF.Sqrt, bias=K_KEPS, scale=K_I128))
                                c_se.wait(VEC)
                                c_ve.inc(VEC.reciprocal(s4[b2][:], s4t[b2][:]))
                                VEC.wait_ge(c_ve.h, tab_cons[par])
                                kc3 = kcp[bk][:, 0:c.CGW].rearrange(
                                    "p (h d) -> p h d", d=128)
                                cg3 = csg[par][:, 0, tt, :].unsqueeze(1).broadcast_to(
                                    [128, c.HCG, 128])
                                t1v = t1b[b2][:, 0:c.CGW].rearrange(
                                    "p (h d) -> p h d", d=128)
                                t2v = t2b[b2][:, 0:c.CGW].rearrange(
                                    "p (h d) -> p h d", d=128)
                                c_ve.inc(VEC.tensor_tensor(t1v, kc3, cg3, ALU.mult))
                                sg_lo = csg[par][:, 1, tt, 0:64].unsqueeze(1).broadcast_to(
                                    [128, c.HCG, 64])
                                sg_hi = csg[par][:, 1, tt, 64:128].unsqueeze(1).broadcast_to(
                                    [128, c.HCG, 64])
                                c_ve.inc(VEC.tensor_tensor(
                                    t2v[:, :, 0:64], kc3[:, :, 64:128], sg_lo, ALU.mult))
                                c_ve.inc(VEC.tensor_tensor(
                                    t2v[:, :, 64:128], kc3[:, :, 0:64], sg_hi, ALU.mult))
                                bank_ve[bk] = c_ve.n
                                vsync()
                                c_ve.inc(VEC.tensor_tensor(
                                    t12b[b2][:, 0:c.CGW], t1b[b2][:, 0:c.CGW],
                                    t2b[b2][:, 0:c.CGW], ALU.add))
                                vsync()
                                VEC.wait_ge(s_sc.h, ot_free[tp])
                                for hh in range(c.HCG):
                                    col = cg * c.CGW + hh * 128
                                    c_ve.inc(VEC.tensor_scalar(
                                        otile[tp][:, col:col + 128],
                                        t12b[b2][:, hh * 128:(hh + 1) * 128],
                                        s4[b2][:, hh:hh + 1], None, ALU.mult))
                            else:
                                c_mm.wait(SCA)
                                SCA.wait_ge(s_sc.h, ot_free[tp])
                                c_se.inc(SCA.copy(
                                    vt[tp][:, cg * c.HCG:(cg + 1) * c.HCG, 0:128], ps3))
                                bank_se[bk] = c_se.n
                        if norm:
                            SCA.wait_ge(c_ve.h, c_ve.n)
                            SCA.wait_ge(s_sc.h, fmst_free[tp])
                            s_sc.inc(SCA.dma_start(
                                fmst[tp][:, :, :],
                                otile[tp][:].rearrange("p (b d) -> p b d", d=128),
                                transpose=True), 16)
                            ot_free[tp] = s_sc.n
                            SCA.wait_ge(s_sc.h, s_sc.n)
                            dst = k_d if kind == "k" else q_d
                            col0 = ft * 128 if kind == "k" else ft * 128 - c.T_EXT
                            s_sc.inc(SCA.dma_start(
                                dst[:, :, col0:col0 + 128].transpose([1, 0, 2]),
                                fmst[tp][:, :, :]), 16)
                            fmst_free[tp] = s_sc.n
                        else:
                            c_se.wait(VEC)
                            VEC.wait_ge(s_sy.h, s_sy.n)
                            VEC.wait_ge(s_sc.h, ot_free[tp])
                            c_ve.inc(VEC.tensor_copy(
                                vt[tp][:, :, 128:129],
                                validc1_sb[:, ft:ft + 1].unsqueeze(1).broadcast_to(
                                    [128, c.NH, 1])))
                            SCA.wait_ge(c_ve.h, c_ve.n)
                            SCA.wait_ge(c_se.h, c_se.n)
                            s_sc.inc(SCA.dma_start(v_d[ft, :, :], vt[tp][:, :, :]), 16)
                            ot_free[tp] = s_sc.n

        proj_pass("k")
        barrier()
        proj_pass("q")
        barrier()
        proj_pass("v")
        barrier()

        ps23 = ExitStack()
        O_sb = _sb(ps23, "O_sb", [128, c.NTT_OWN, c.NH, 128], BF16)
        with ExitStack() as ps:
            kwin = [_sb(ps, f"kwin{i}", [128, c.W + 512], BF16) for i in range(2)]
            vwin = [_sb(ps, f"vwin{i}", [128, c.MT, 129], BF16) for i in range(2)]
            qt = [_sb(ps, f"qt{i}", [128, 512], BF16) for i in range(2)]
            NSP = 4
            pT = [_sb(ps, f"pT{i}", [128, 512], BF16) for i in range(NSP)]
            rl_sb = _sb(ps, "rl_sb", [128, 1], F32)
            SP = [_pt(ps, f"SP{i}", [128, 512])
                  for i in range(NSP)]
            OP = [_pt(ps, f"OP{i}", [128, 129])
                  for i in range(4)]

            NIT = c.QC * c.NH
            win_cons = [0, 0]
            sp_free = [0] * NSP
            pt_free = [0] * NSP
            ptm_free = [0] * NSP
            op_free = [0] * 4

            def issue_loads(it):
                qc, h = divmod(it, c.NH)
                pi = it % 2
                SYN.wait_ge(c_mm.h, win_cons[pi])
                ld = sp[pi]
                ld.inc(SYN.dma_start(
                    kwin[pi][:], k_d[h, :, qc * 512:qc * 512 + c.W + 512]), 16)
                ld.inc(SYN.dma_start(
                    vwin[pi][:],
                    v_d[qc * 4:qc * 4 + c.MT, :, h * 129:(h + 1) * 129].transpose(
                        [1, 0, 2])), 16)
                ld.inc(SYN.dma_start(qt[pi][:], q_d[h, :, qc * 512:(qc + 1) * 512]), 16)
                return ld.n

            load_n = [0] * NIT
            load_n[0] = issue_loads(0)
            if NIT > 1:
                load_n[1] = issue_loads(1)

            for it in range(NIT):
                qc, h = divmod(it, c.NH)
                pi = it % 2
                TEN.wait_ge(sp[pi].h, load_n[it])
                exp_ns = {}

                def emit_S(m):
                    bk = m % NSP
                    jlo = max(0, m - c.WT)
                    jhi = min(3, m)
                    TEN.wait_ge(c_se.h, sp_free[bk])
                    TEN.wait_ge(c_mm.h, ptm_free[bk]) if False else None
                    ins = TEN.matmul(
                        SP[bk][:, jlo * 128:(jhi + 1) * 128],
                        kwin[pi][:, m * 128:(m + 1) * 128],
                        qt[pi][:, jlo * 128:(jhi + 1) * 128],
                        start=True, stop=True)
                    c_mm.inc(ins)
                    c_mm.wait(SCA)
                    SCA.wait_ge(c_mm.h, pt_free[bk])
                    c_se.inc(SCA.activation(
                        pT[bk][:, jlo * 128:(jhi + 1) * 128],
                        SP[bk][:, jlo * 128:(jhi + 1) * 128], AF.Exp))
                    exp_ns[m] = c_se.n
                    sp_free[bk] = c_se.n
                    if 0 <= m - c.WT <= 3:
                        j = m - c.WT
                        c_se.wait(VEC)
                        VEC.wait_ge(c_mm.h, ptm_free[bk])
                        c_ve.inc(VEC.tensor_tensor(
                            pT[bk][:, j * 128:(j + 1) * 128],
                            pT[bk][:, j * 128:(j + 1) * 128], maskd_sb[:], ALU.mult))
                        exp_ns[m] = ("ve", c_ve.n)
                    elif m <= 3:
                        j = m
                        c_se.wait(VEC)
                        VEC.wait_ge(c_mm.h, ptm_free[bk])
                        c_ve.inc(VEC.tensor_tensor(
                            pT[bk][:, j * 128:(j + 1) * 128],
                            pT[bk][:, j * 128:(j + 1) * 128], maskw_sb[:], ALU.mult))
                        exp_ns[m] = ("ve", c_ve.n)

                def emit_AV(m):
                    bk = m % NSP
                    jlo = max(0, m - c.WT)
                    jhi = min(3, m)
                    en = exp_ns[m]
                    if isinstance(en, tuple):
                        TEN.wait_ge(c_ve.h, en[1])
                    else:
                        TEN.wait_ge(c_se.h, en)
                    for j in range(jlo, jhi + 1):
                        if m == j:
                            TEN.wait_ge(c_ve.h, op_free[j])
                        ins = TEN.matmul(
                            OP[j][:],
                            pT[bk][:, j * 128:(j + 1) * 128],
                            vwin[pi][:, m, :],
                            start=(m == j), stop=(m == j + c.WT))
                    c_mm.inc(ins)
                    pt_free[bk] = c_mm.n
                    ptm_free[bk] = c_mm.n

                for m in range(c.MT):
                    emit_S(m)
                    if m >= 2:
                        emit_AV(m - 2)
                emit_AV(c.MT - 2)
                emit_AV(c.MT - 1)
                win_cons[pi] = c_mm.n
                for j in range(4):
                    c_mm.wait(VEC)
                    c_ve.inc(VEC.reciprocal(rl_sb[:], OP[j][:, 128:129]))
                    vsync()
                    c_ve.inc(VEC.tensor_scalar(
                        O_sb[:, qc * 4 + j, h, :], OP[j][:, 0:128], rl_sb[:, 0:1],
                        None, ALU.mult))
                    op_free[j] = c_ve.n
                if it + 2 < NIT:
                    load_n[it + 2] = issue_loads(it + 2)

        barrier()

        with ExitStack() as ps:
            wout_sb = _sb(ps, "wout_sb", [128, c.DC, c.D], BF16)
            ofm = [_sb(ps, f"ofm{i}", [128, c.NH, 128], BF16) for i in range(2)]
            x_f = [_sb(ps, f"x3f{i}", [128, c.D], F32) for i in range(2)]
            xp_sb = [_sb(ps, f"xp{i}", [128, c.D], F32) for i in range(2)]
            xsq3 = _sb(ps, "xsq3", [128, c.D], BF16)
            xn2t = [_sb(ps, f"xn2t{i}", [128, c.D], BF16) for i in range(2)]
            fm2 = [_sb(ps, f"fm2_{i}", [128, c.DC, 128], BF16) for i in range(2)]
            lnst3 = _sb(ps, "lnst3", [128, 8], F32)
            pp3 = [_pt(ps, f"pp3{i}", [128, 512])
                   for i in range(4)]
            for cc in range(c.DC):
                s_gl.inc(GPS.dma_start(wout_sb[:, cc, :],
                                       wout[cc * 128:(cc + 1) * 128, :]), 16)
            wres_n = s_gl.n

            ofm_free = [0, 0]
            xf_free = [0, 0]
            xp_free = [0, 0]
            bank_ve3 = [0, 0, 0, 0]

            for tt in range(c.NTT_OWN):
                tp = tt % 2
                SCA.wait_ge(c_mm.h, ofm_free[tp])
                s_sc.inc(SCA.dma_start(ofm[tp][:, :, :], O_sb[:, tt, :, :],
                                       transpose=True), 16)
                ofm_n = s_sc.n
                SYN.wait_ge(c_ve.h, xf_free[tp])
                ld = sp[tp]
                ld.inc(SYN.dma_start(
                    x_f[tp][:],
                    x_pad[c.T_EXT + tt * 128:c.T_EXT + (tt + 1) * 128, :]), 16)
                for oc in range(c.NOC):
                    bk = (tt * c.NOC + oc) % 4
                    TEN.wait_ge(s_sc.h, ofm_n)
                    TEN.wait_ge(s_gl.h, wres_n)
                    TEN.wait_ge(c_ve.h, bank_ve3[bk])
                    for cc in range(c.DC):
                        ins = TEN.matmul(
                            pp3[bk][:, 0:c.OCW], ofm[tp][:, cc, :],
                            wout_sb[:, cc, oc * c.OCW:(oc + 1) * c.OCW],
                            start=(cc == 0), stop=(cc == c.DC - 1))
                    c_mm.inc(ins)
                    ofm_free[tp] = c_mm.n
                    c_mm.wait(VEC)
                    VEC.wait_ge(ld.h, ld.n)
                    VEC.wait_ge(s_sc.h, xp_free[tp])
                    c_ve.inc(VEC.tensor_tensor(
                        xp_sb[tp][:, oc * c.OCW:(oc + 1) * c.OCW], pp3[bk][:, 0:c.OCW],
                        x_f[tp][:, oc * c.OCW:(oc + 1) * c.OCW], ALU.add))
                    bank_ve3[bk] = c_ve.n
                xf_free[tp] = c_ve.n
                c_ve.wait(SCA)
                vsync()
                emit_ln(xp_sb[tp][:], xsq3[:], lnst3, xn2t[tp][:])
                SCA.wait_ge(c_ve.h, c_ve.n)
                SCA.wait_ge(s_sc.h, xp_free[tp])
                s_sc.inc(SCA.dma_start(xp_d[tt * 128:(tt + 1) * 128, :], xp_sb[tp][:]), 16)
                s_sc.inc(SCA.dma_start(xn2t_d[tt * 128:(tt + 1) * 128, :], xn2t[tp][:]), 16)
                s_sc.inc(SCA.dma_start(
                    fm2[tp][:, :, :],
                    xn2t[tp][:].rearrange("p (b d) -> p b d", d=128),
                    transpose=True), 16)
                SCA.wait_ge(s_sc.h, s_sc.n)
                s_sc.inc(SCA.dma_start(
                    xn2_d[:, :, tt * 128:(tt + 1) * 128].transpose([1, 0, 2]),
                    fm2[tp][:, :, :]), 16)
                xp_free[tp] = s_sc.n

        ps23.close()
        barrier()

        I32 = mybir.dt.int32
        IOA = bass.IndirectOffsetOnAxis
        with ExitStack() as ps:
            wr_sb = _sb(ps, "wr_sb", [128, c.DC, c.E], BF16)
            triu_sb = _sb(ps, "triu_sb", [128, 128], BF16)
            iota_sb = _sb(ps, "iota_sb", [128, c.NTT_OWN], I32)
            wf_res = _sb(ps, "wf_res", [128, c.NTT_OWN, c.E], F32)
            for cc in range(c.DC):
                s_gl.inc(GPS.dma_start(wr_sb[:, cc, :],
                                       wr_t[cc * 128:(cc + 1) * 128, :]), 16)
            s_gl.inc(GPS.dma_start(triu_sb[:], triu[:, :]), 16)
            s_gl.inc(GPS.dma_start(iota_sb[:], iota_h[:, :]), 16)
            wr_n = s_gl.n
            with ExitStack() as zs:
                zer_sb = _sb(zs, "zer_sb", [128, c.D], F32)
                c_ve.inc(VEC.memset(zer_sb[:], 0.0))
                SCA.wait_ge(c_ve.h, c_ve.n)
                for tl in range(c.NTT_OWN):
                    for rb in range(2):
                        s_sc.inc(SCA.dma_start(
                            re_d[rb][tl * 128:(tl + 1) * 128, 0:c.D],
                            zer_sb[:, 0:c.D]), 16)
                re_zero_n = s_sc.n
                for _eng in ALL_ENG:
                    _eng.wait_ge(s_sc.h, re_zero_n)

            with ExitStack() as rs:
                xnc = [_sb(rs, f"xnc{i}", [128, c.DC, 512], BF16) for i in range(2)]
                sm_sb = _sb(rs, "sm_sb", [128, c.E], F32)
                st_sb = _sb(rs, "st_sb", [128, 6], F32)
                pr = _pt(rs, "pr", [128, c.E])
                xnc_cons = [0, 0]
                pr_free = (0, 0)
                for chq in range(c.T_OWN // 512):
                    par = chq % 2
                    SYN.wait_ge(c_mm.h, xnc_cons[par])
                    ld = sq[par]
                    ld.inc(SYN.dma_start(
                        xnc[par][:, :, :],
                        xn2_d[:, :, chq * 512:(chq + 1) * 512].transpose([1, 0, 2])), 16)
                    for st4 in range(4):
                        tl = chq * 4 + st4
                        TEN.wait_ge(ld.h, ld.n)
                        TEN.wait_ge(s_gl.h, wr_n)
                        TEN.wait_ge(c_ve.h, pr_free[0])
                        TEN.wait_ge(c_se.h, pr_free[1])
                        for cc in range(c.DC):
                            ins = TEN.matmul(
                                pr[:], xnc[par][:, cc, st4 * 128:(st4 + 1) * 128],
                                wr_sb[:, cc, :], start=(cc == 0),
                                stop=(cc == c.DC - 1))
                        c_mm.inc(ins)
                        xnc_cons[par] = c_mm.n
                        c_mm.wait(VEC)
                        c_ve.inc(VEC.reduce_max(st_sb[:, 0:1], pr[:], axis=AX.X,
                                                negate=True))
                        c_ve.wait(SCA)
                        c_se.inc(SCA.activation(sm_sb[:], pr[:], AF.Exp,
                                                bias=st_sb[:, 0:1]))
                        pr_free = (c_ve.n, c_se.n)
                        c_se.wait(VEC)
                        c_ve.inc(VEC.reduce_sum(st_sb[:, 1:2], sm_sb[:], axis=AX.X))
                        vsync()
                        c_ve.inc(VEC.reciprocal(st_sb[:, 2:3], st_sb[:, 1:2]))
                        vsync()
                        c_ve.inc(VEC.tensor_scalar(sm_sb[:], sm_sb[:], st_sb[:, 2:3],
                                                   None, ALU.mult))
                        vsync()
                        c_ve.inc(VEC.reduce_max(st_sb[:, 3:4], sm_sb[:], axis=AX.X,
                                                negate=True))
                        vsync()
                        c_ve.inc(VEC.tensor_scalar(wf_res[:, tl, :], sm_sb[:],
                                                   st_sb[:, 3:4], 0.0, ALU.add,
                                                   ALU.is_ge))
                        vsync()
                        c_ve.inc(VEC.scalar_tensor_tensor(
                            wf_res[:, tl, :], wf_res[:, tl, :], NEG, sm_sb[:],
                            ALU.mult, ALU.add))
                        vsync()
                        c_ve.inc(VEC.reduce_max(st_sb[:, 4:5], wf_res[:, tl, :],
                                                axis=AX.X, negate=True))
                        vsync()
                        c_ve.inc(VEC.tensor_scalar(wf_res[:, tl, :], sm_sb[:],
                                                   st_sb[:, 4:5], 0.0, ALU.add,
                                                   ALU.is_ge))
                        vsync()
                        c_ve.inc(VEC.tensor_tensor(wf_res[:, tl, :], wf_res[:, tl, :],
                                                   sm_sb[:], ALU.mult))
                SCA.wait_ge(c_ve.h, c_ve.n)
                s_sc.inc(SCA.dma_start(
                    wf_d[:, :].rearrange("(a p) e -> p a e", p=128), wf_res[:]), 16)
                wf_store_n = s_sc.n

            with ExitStack() as es:
                NSUB = c.NSUB
                xe = _sb(es, "xe", [128, NSUB, c.D], BF16)
                xefm = _sb(es, "xefm", [128, c.DC, c.CAP], BF16)
                geu = _sb(es, "geu", [128, c.NF, c.CAP], BF16)
                gs_sb = [_sb(es, f"gs4{i}", [128, 512], BF16) for i in range(2)]
                wa_sb = [_sb(es, f"wa4{i}", [128, c.DC, 256], BF16) for i in range(3)]
                wbe = [_sb(es, f"wbe4{i}", [128, c.NF, c.D], BF16) for i in range(2)]
                hsc = [_sb(es, f"hsc{i}", [128, c.D], F32) for i in range(2)]
                idx_sb = [_sb(es, f"idx_sb{i}", [128, NSUB], I32) for i in range(2)]
                wfe = [_sb(es, f"wfe{i}", [128, NSUB, c.E], F32) for i in range(2)]
                oob_sb = _sb(es, "oob_sb", [128, NSUB], I32)
                msk = _sb(es, "msk", [128, c.NTT_OWN], F32)
                csA = _sb(es, "csA", [128, c.NTT_OWN], F32)
                csB = _sb(es, "csB", [128, c.NTT_OWN], F32)
                rsum = _sb(es, "rsum", [128, 1], F32)
                rsum_bf = _sb(es, "rsum_bf", [128, 1], BF16)
                posb = [_sb(es, f"posb{i}", [128, c.NTT_OWN], F32) for i in range(2)]
                posi = [_sb(es, f"posi{i}", [128, c.NTT_OWN], I32) for i in range(2)]
                pg = [_pt(es, f"pg4{i}", [128, 512]) for i in range(2)]
                pu = [_pt(es, f"pu4{i}", [128, 512]) for i in range(2)]
                pa = [_pt(es, f"pa4{i}", [128, 512]) for i in range(2)]
                pcs = _pt(es, "pcs", [128, 1])

                c_ve.inc(VEC.memset(oob_sb[:], 1 << 30))
                c_ve.inc(VEC.memset(xe[:], 0.0))
                rg_cap = es.enter_context(GPS.register("rg_cap"))
                rg_town = es.enter_context(GPS.register("rg_town"))
                GPS.reg_mov(rg_cap, c.CAP)
                GPS.reg_mov(rg_town, c.T_OWN)

                t5s = []
                off = 0
                while off < c.CAP:
                    t5s.append((off, min(512, c.CAP - off)))
                    off += 512

                geu_cons = [0]
                prep_done = [0, 0]
                xefm_cons = [0, 0]
                wfe_cons = [0, 0]
                xe_free = [0]
                pcs_free = [0]
                scat_n = [0]
                scat_pp = [0, 0]
                wa_cons = [0, 0, 0]
                wbe_cons = [0, 0]
                pg_free = [0, 0]
                pu_free = [0, 0]
                pa_free = [0, 0]

                def prep(e):
                    ep = e % 2
                    VEC.wait_ge(s_sc.h, wf_store_n)
                    VEC.wait_ge(sr[ep].h, prep_done[ep])
                    c_ve.inc(VEC.tensor_scalar(
                        msk[:], wf_res[:, :, e], 0.0, None, ALU.is_gt))
                    vsync()
                    c_ve.inc(VEC.reduce_sum(rsum[:], msk[:], axis=AX.X))
                    vsync()
                    c_ve.inc(VEC.tensor_copy(rsum_bf[:], rsum[:]))
                    c_ve.wait(TEN)
                    TEN.wait_ge(s_gl.h, wr_n)
                    TEN.wait_ge(c_ve.h, pcs_free[0])
                    c_mm.inc(TEN.matmul(pcs[:], triu_sb[:], rsum_bf[:],
                                        start=True, stop=True))
                    c_mm.wait(VEC)
                    c_ve.inc(VEC.tensor_copy(csA[:], msk[:]))
                    sh = 1
                    cur, nxt = csA, csB
                    while sh < c.NTT_OWN:
                        vsync()
                        c_ve.inc(VEC.tensor_copy(nxt[:, 0:sh], cur[:, 0:sh]))
                        c_ve.inc(VEC.tensor_tensor(
                            nxt[:, sh:c.NTT_OWN], cur[:, 0:c.NTT_OWN - sh],
                            cur[:, sh:c.NTT_OWN], ALU.add))
                        cur, nxt = nxt, cur
                        sh *= 2
                    vsync()
                    c_ve.inc(VEC.tensor_tensor(cur[:], cur[:], msk[:], ALU.subtract))
                    vsync()
                    c_ve.inc(VEC.scalar_tensor_tensor(
                        posb[ep][:], rsum[:].broadcast_to([128, c.NTT_OWN]), -1.0,
                        pcs[:].broadcast_to([128, c.NTT_OWN]), ALU.mult, ALU.add))
                    pcs_free[0] = c_ve.n
                    vsync()
                    c_ve.inc(VEC.tensor_tensor(posb[ep][:], posb[ep][:], cur[:],
                                               ALU.add))
                    vsync()
                    c_ve.inc(VEC.tensor_scalar(csB[:], msk[:], -1e9, 1e9,
                                               ALU.mult, ALU.add))
                    vsync()
                    c_ve.inc(VEC.tensor_tensor(posb[ep][:], posb[ep][:], csB[:],
                                               ALU.add))
                    vsync()
                    c_ve.inc(VEC.tensor_copy(posi[ep][:], posb[ep][:]))
                    pos_ready = c_ve.n
                    lw = sr[ep]
                    GPS.wait_ge(lw.h, prep_done[ep])
                    lw.inc(GPS.dma_start(idx_d[ep][:, :], oob_sb[:]), 16)
                    GPS.wait_ge(lw.h, lw.n)
                    GPS.wait_ge(c_ve.h, pos_ready)
                    odst = idx_d[ep].rearrange("p a -> (p a)").unsqueeze(1)
                    for tl in range(c.NTT_OWN):
                        lw.inc(GPS.indirect_dma_start(
                            odst, IOA(posi[ep][:, tl:tl + 1], 0),
                            iota_sb[:, tl:tl + 1], None,
                            bounds_check=rg_cap, oob_is_err=False), 16)
                    GPS.wait_ge(lw.h, lw.n)
                    GPS.wait_ge(s_gl.h, scat_n[0])
                    lw.inc(GPS.dma_start(idx_sb[ep][:, :], idx_d[ep][:, :]), 16)
                    GPS.wait_ge(lw.h, lw.n)
                    GPS.wait_ge(s_sc.h, xe_free[0])
                    GPS.wait_ge(c_ve.h, wfe_cons[ep])
                    for i in range(NSUB):
                        lw.inc(GPS.indirect_dma_start(
                            xe[:, i, :], None, xn2t_d[:, :],
                            IOA(idx_sb[ep][:, i:i + 1], 0),
                            bounds_check=rg_town, oob_is_err=False), 16)
                        lw.inc(GPS.indirect_dma_start(
                            wfe[ep][:, i, :], None, wf_d[:, :],
                            IOA(idx_sb[ep][:, i:i + 1], 0),
                            bounds_check=rg_town, oob_is_err=False), 16)
                    gather_n = lw.n
                    prep_done[ep] = lw.n
                    return gather_n, lw.n

                def prep_b(e, gather_n):
                    ep = e % 2
                    SCA.wait_ge(sr[ep].h, gather_n)
                    SCA.wait_ge(c_mm.h, xefm_cons[0])
                    for i in range(NSUB):
                        s_sc.inc(SCA.dma_start(
                            xefm[:, :, i * 128:(i + 1) * 128],
                            xe[:, i, :].rearrange("p (b d) -> p b d", d=128),
                            transpose=True), 16)
                    xe_free[0] = s_sc.n
                    return s_sc.n

                def compute(e, xf_n, prep_lw, nxt_gather):
                    ep = e % 2
                    wp = e % 2
                    GPS.wait_ge(c_mm.h, wbe_cons[wp])
                    lwb = sq[wp]
                    lwb.inc(GPS.dma_start(
                        wbe[wp][:, :, :],
                        wd_t[e].rearrange("(ftt p) d -> p ftt d", p=128)), 16)
                    wbe_n = lwb.n
                    for ft in range(c.NF):
                        fp = ft % 3
                        GPS.wait_ge(c_mm.h, wa_cons[fp])
                        lw = sr[ft % 2]
                        lw.inc(GPS.dma_start(
                            wa_sb[fp][:, :, 0:128],
                            wg_t[e][:, ft * 128:(ft + 1) * 128].rearrange(
                                "(cc p) f -> p cc f", p=128)), 16)
                        lw.inc(GPS.dma_start(
                            wa_sb[fp][:, :, 128:256],
                            wu_t[e][:, ft * 128:(ft + 1) * 128].rearrange(
                                "(cc p) f -> p cc f", p=128)), 16)
                        my_n = lw.n
                        for t0_, tw in t5s:
                            bk = (t0_ // 512) % 2
                            TEN.wait_ge(lw.h, my_n)
                            TEN.wait_ge(s_sc.h, xf_n)
                            TEN.wait_ge(c_se.h, pg_free[bk])
                            TEN.wait_ge(c_ve.h, pu_free[bk])
                            for cc in range(c.DC):
                                TEN.matmul(
                                    pg[bk][:, 0:tw], wa_sb[fp][:, cc, 0:128],
                                    xefm[:, cc, t0_:t0_ + tw],
                                    start=(cc == 0), stop=(cc == c.DC - 1))
                            for cc in range(c.DC):
                                ins = TEN.matmul(
                                    pu[bk][:, 0:tw], wa_sb[fp][:, cc, 128:256],
                                    xefm[:, cc, t0_:t0_ + tw],
                                    start=(cc == 0), stop=(cc == c.DC - 1))
                            c_mm.inc(ins)
                            wa_cons[fp] = c_mm.n
                            xefm_cons[0] = c_mm.n
                            c_mm.wait(SCA)
                            c_se.inc(SCA.activation(gs_sb[bk][:, 0:tw],
                                                    pg[bk][:, 0:tw], AF.Silu))
                            pg_free[bk] = c_se.n
                            c_se.wait(VEC)
                            VEC.wait_ge(c_mm.h, geu_cons[0])
                            c_ve.inc(VEC.tensor_tensor(
                                geu[:, ft, t0_:t0_ + tw], gs_sb[bk][:, 0:tw],
                                pu[bk][:, 0:tw], ALU.mult))
                            pu_free[bk] = c_ve.n
                    up_done = c_ve.n
                    if nxt_gather is not None:
                        nxt_xf = prep_b(e + 1, nxt_gather)
                    else:
                        nxt_xf = 0
                    for i in range(NSUB):
                        hp = i % 2
                        for oc in range(c.NOC):
                            bk = oc % 2
                            TEN.wait_ge(lwb.h, wbe_n)
                            TEN.wait_ge(c_ve.h, max(up_done, pa_free[bk]))
                            for ftt in range(c.NF):
                                ins = TEN.matmul(
                                    pa[bk][:, 0:c.OCW],
                                    geu[:, ftt, i * 128:(i + 1) * 128],
                                    wbe[wp][:, ftt, oc * c.OCW:(oc + 1) * c.OCW],
                                    start=(ftt == 0), stop=(ftt == c.NF - 1))
                            c_mm.inc(ins)
                            wbe_cons[wp] = c_mm.n
                            geu_cons[0] = c_mm.n
                            c_mm.wait(VEC)
                            VEC.wait_ge(sr[ep].h, prep_lw)
                            VEC.wait_ge(s_gl.h, scat_n[0])
                            c_ve.inc(VEC.tensor_scalar(
                                hsc[hp][:, oc * c.OCW:(oc + 1) * c.OCW],
                                pa[bk][:, 0:c.OCW], wfe[ep][:, i, e:e + 1], None,
                                ALU.mult))
                            pa_free[bk] = c_ve.n
                        wfe_cons[ep] = c_ve.n
                        GPS.wait_ge(c_ve.h, c_ve.n)
                        GPS.wait_ge(s_sc.h, re_zero_n)
                        GPS.wait_ge(s_gl.h, scat_pp[ep])
                        s_gl.inc(GPS.indirect_dma_start(
                            re_d[ep][:, :], IOA(idx_sb[ep][:, i:i + 1], 0),
                            hsc[hp][:, :], None,
                            bounds_check=rg_town, oob_is_err=False,
                            compute_op=ALU.add), 16)
                        scat_n[0] = s_gl.n
                    scat_pp[ep] = s_gl.n
                    return nxt_xf

                SERIAL = False
                if SERIAL:
                    E_RUN = getattr(cfg, "e_run", c.E)
                    for e in range(E_RUN):
                        pn = prep(e)
                        barrier()
                        xf_n = prep_b(e, pn[0])
                        barrier()
                        compute(e, xf_n, pn[1], None)
                        barrier()
                else:
                    pn = prep(0)
                    xf_n = prep_b(0, pn[0])
                    for e in range(c.E):
                        nxt = prep(e + 1) if e + 1 < c.E else None
                        xf_n = compute(e, xf_n, pn[1],
                                       nxt[0] if nxt is not None else None)
                        pn = nxt if nxt is not None else pn

            barrier()

            for half in range(2):
                t0 = half * c.HALF
                with ExitStack() as hs:
                    xn2h = _sb(hs, "xn2h", [128, c.DC, c.HALF], BF16)
                    sgu = _sb(hs, "sgu", [128, c.NFS, c.HALF], BF16)
                    gs_sb = [_sb(hs, f"gs{i}", [128, 512], BF16) for i in range(2)]
                    wa_sb = [_sb(hs, f"wa{i}", [128, c.DC, 256], BF16)
                             for i in range(2)]
                    wbs = [_sb(hs, f"wbs{i}", [128, c.NFS, c.OCW], BF16)
                           for i in range(2)]
                    xr_t = [_sb(hs, f"xr{i}", [128, c.D], F32) for i in range(2)]
                    re_t = [_sb(hs, f"re{i}", [128, c.D], F32) for i in range(2)]
                    out_t = [_sb(hs, f"out{i}", [128, c.D], F32) for i in range(2)]
                    pg = [_pt(hs, f"pg{i}", [128, 512]) for i in range(2)]
                    pu = [_pt(hs, f"pu{i}", [128, 512]) for i in range(2)]
                    pa = [_pt(hs, f"pa{i}", [128, 512]) for i in range(2)]

                    ldh = sq[half % 2]
                    SYN.wait_ge(c_mm.h, c_mm.n)
                    ldh.inc(SYN.dma_start(
                        xn2h[:, :, :],
                        xn2_d[:, :, t0:t0 + c.HALF].transpose([1, 0, 2])), 16)
                    ldh_n = ldh.n

                    wa_cons = [0, 0]
                    wbs_cons = [0, 0]
                    pg_free = [0, 0]
                    pu_free = [0, 0]
                    pa_free = [0, 0]
                    sgu_cons = 0

                    for ft in range(c.NFS):
                        fp = ft % 2
                        GPS.wait_ge(c_mm.h, wa_cons[fp])
                        lw = sr[fp]
                        lw.inc(GPS.dma_start(
                            wa_sb[fp][:, :, 0:128],
                            wsg_t[:, ft * 128:(ft + 1) * 128].rearrange(
                                "(cc p) f -> p cc f", p=128)), 16)
                        lw.inc(GPS.dma_start(
                            wa_sb[fp][:, :, 128:256],
                            wsu_t[:, ft * 128:(ft + 1) * 128].rearrange(
                                "(cc p) f -> p cc f", p=128)), 16)
                        my_n = lw.n
                        for t5 in range(c.NT5):
                            bk = t5 % 2
                            TEN.wait_ge(lw.h, my_n)
                            TEN.wait_ge(ldh.h, ldh_n)
                            TEN.wait_ge(c_se.h, pg_free[bk])
                            TEN.wait_ge(c_ve.h, pu_free[bk])
                            for cc in range(c.DC):
                                TEN.matmul(
                                    pg[bk][:], wa_sb[fp][:, cc, 0:128],
                                    xn2h[:, cc, t5 * 512:(t5 + 1) * 512],
                                    start=(cc == 0), stop=(cc == c.DC - 1))
                            for cc in range(c.DC):
                                ins = TEN.matmul(
                                    pu[bk][:], wa_sb[fp][:, cc, 128:256],
                                    xn2h[:, cc, t5 * 512:(t5 + 1) * 512],
                                    start=(cc == 0), stop=(cc == c.DC - 1))
                            c_mm.inc(ins)
                            wa_cons[fp] = c_mm.n
                            c_mm.wait(SCA)
                            c_se.inc(SCA.activation(gs_sb[bk][:], pg[bk][:], AF.Silu))
                            pg_free[bk] = c_se.n
                            c_se.wait(VEC)
                            c_ve.inc(VEC.tensor_tensor(
                                sgu[:, ft, t5 * 512:(t5 + 1) * 512], gs_sb[bk][:],
                                pu[bk][:], ALU.mult))
                            pu_free[bk] = c_ve.n
                    up_done = c_ve.n

                    xrre_free = [0, 0]
                    out_free = [0, 0]
                    if True:
                        xr_free = [0, 0]
                        for tl in range(c.NTH):
                            tpp = tl % 2
                            row0 = t0 + tl * 128
                            SYN.wait_ge(s_sc.h, xr_free[tpp])
                            ldx = sp[tpp]
                            ldx.inc(SYN.dma_start(
                                xr_t[tpp][:], xp_d[row0:row0 + 128, :]), 16)
                            ldx.inc(SYN.dma_start(
                                re_t[tpp][:], re_d[0][row0:row0 + 128, :]), 16)
                            VEC.wait_ge(ldx.h, ldx.n)
                            c_ve.inc(VEC.tensor_tensor(
                                xr_t[tpp][:], xr_t[tpp][:], re_t[tpp][:], ALU.add))
                            SYN.wait_ge(c_ve.h, c_ve.n)
                            ldx.inc(SYN.dma_start(
                                re_t[tpp][:], re_d[1][row0:row0 + 128, :]), 16)
                            VEC.wait_ge(ldx.h, ldx.n)
                            c_ve.inc(VEC.tensor_tensor(
                                xr_t[tpp][:], xr_t[tpp][:], re_t[tpp][:], ALU.add))
                            SCA.wait_ge(c_ve.h, c_ve.n)
                            s_sc.inc(SCA.dma_start(
                                re_d[0][row0:row0 + 128, :], xr_t[tpp][:]), 16)
                            xr_free[tpp] = s_sc.n
                        xr_done = s_sc.n

                    for oc in range(c.NOC):
                        ow = oc % 2
                        GPS.wait_ge(c_mm.h, wbs_cons[ow])
                        lw = sr[oc % 2]
                        lw.inc(GPS.dma_start(
                            wbs[ow][:, 0:c.NFS, :],
                            wsd_t[:, oc * c.OCW:(oc + 1) * c.OCW].rearrange(
                                "(ftt p) d -> p ftt d", p=128)), 16)
                        my_n = lw.n
                        for tl in range(c.NTH):
                            tpp = tl % 2
                            bk = tl % 2
                            row0 = t0 + tl * 128
                            SYN.wait_ge(s_sc.h, xr_done)
                            SYN.wait_ge(c_ve.h, xrre_free[tpp])
                            ldx = sp[tpp]
                            ldx.inc(SYN.dma_start(
                                re_t[tpp][:, 0:c.OCW],
                                re_d[0][row0:row0 + 128, oc * c.OCW:(oc + 1) * c.OCW]),
                                16)
                            TEN.wait_ge(lw.h, my_n)
                            TEN.wait_ge(c_ve.h, max(up_done, pa_free[bk]))
                            for ftt in range(c.NFS):
                                ins = TEN.matmul(
                                    pa[bk][:, 0:c.OCW],
                                    sgu[:, ftt, tl * 128:(tl + 1) * 128],
                                    wbs[ow][:, ftt, :],
                                    start=(ftt == 0), stop=(ftt == c.NFS - 1))
                            c_mm.inc(ins)
                            wbs_cons[ow] = c_mm.n
                            c_mm.wait(VEC)
                            VEC.wait_ge(ldx.h, ldx.n)
                            VEC.wait_ge(s_out.h, out_free[tpp])
                            c_ve.inc(VEC.tensor_tensor(
                                out_t[tpp][:, 0:c.OCW], pa[bk][:, 0:c.OCW],
                                re_t[tpp][:, 0:c.OCW], ALU.add))
                            pa_free[bk] = c_ve.n
                            xrre_free[tpp] = c_ve.n
                            SYN.wait_ge(c_ve.h, c_ve.n)
                            s_out.inc(SYN.dma_start(
                                out[row0:row0 + 128, oc * c.OCW:(oc + 1) * c.OCW],
                                out_t[tpp][:, 0:c.OCW]), 16)
                            out_free[tpp] = s_out.n


        if getattr(cfg, "debug", False):
            barrier()
            for name, src in [("dbg_xn1", xn1_d), ("dbg_q", q_d), ("dbg_k", k_d),
                              ("dbg_v", v_d), ("dbg_xp", xp_d), ("dbg_xn2", xn2_d),
                              ("dbg_re", re_d[0]), ("dbg_wf", wf_d),
                              ("dbg_xn2t", xn2t_d), ("dbg_idx0", idx_d[0])]:
                dst = nc.dram_tensor(name, list(src.tensor.shape), src.dtype,
                                     kind="ExternalOutput").ap()
                s_out.inc(SYN.dma_start(dst, src), 16)

        SYN.wait_ge(s_out.h, s_out.n)
        GPS.wait_ge(s_out.h, s_out.n)
    return nc


def _host_inputs(cfg, core_id, n_chunks, inp):
    c = cfg
    b = core_id // n_chunks
    cq = core_id % n_chunks
    x = np.asarray(inp["x"])
    x_pad = np.zeros((c.T_FRAME, c.D), np.float32)
    lo = cq * c.T_OWN - c.T_EXT
    src_lo = max(0, lo)
    x_pad[src_lo - lo:, :] = x[b, src_lo:(cq + 1) * c.T_OWN, :]

    pad_rows = max(0, -lo)
    pad_tiles = pad_rows // 128
    validc1 = np.zeros((128, c.NTT), np.float32)
    validc1[:, pad_tiles:] = 1.0

    pos = np.maximum(np.arange(lo, (cq + 1) * c.T_OWN), 0).astype(np.float64)
    invf = 1.0 / THETA ** (np.arange(0, 128, 2, dtype=np.float64)[:64] / 128.0)
    freqs = pos[:, None] * invf[None, :]
    emb = np.concatenate([freqs, freqs], axis=1)
    cos_tab = np.cos(emb).astype(np.float32)
    sin_tab = np.sin(emb).astype(np.float32)
    sin_tab[:, 0:64] *= -1.0

    ii = np.arange(128)
    maskd = np.where(ii[None, :] >= ii[:, None], 1.0, 0.0).astype(BFNP)
    maskw = np.where(ii[None, :] <= ii[:, None], 1.0, 0.0).astype(BFNP)
    identm = np.eye(128, dtype=np.float32).astype(BFNP)
    triu_m = np.where(ii[:, None] <= ii[None, :], 1.0, 0.0).astype(BFNP)
    iota_m = (np.arange(c.T_OWN, dtype=np.int32)
              .reshape(c.NTT_OWN, 128).T.copy())

    return {
        "x_pad": x_pad,
        "wqkv": inp["w_qkv"],
        "wout": inp["w_out"],
        "wr_t": np.ascontiguousarray(np.asarray(inp["Wr"]).T),
        "wsg_t": np.ascontiguousarray(np.asarray(inp["Wsg"]).T),
        "wsu_t": np.ascontiguousarray(np.asarray(inp["Wsu"]).T),
        "wsd_t": np.ascontiguousarray(np.asarray(inp["Wsd"]).T),
        "wg_t": np.ascontiguousarray(np.asarray(inp["Wg"]).transpose(0, 2, 1)),
        "wu_t": np.ascontiguousarray(np.asarray(inp["Wu"]).transpose(0, 2, 1)),
        "wd_t": np.ascontiguousarray(np.asarray(inp["Wd"]).transpose(0, 2, 1)),
        "gq_row": np.asarray(inp["q_gamma"]).reshape(1, 128),
        "gk_row": np.asarray(inp["k_gamma"]).reshape(1, 128),
        "cos_t": cos_tab,
        "sin_t": sin_tab,
        "maskd": maskd,
        "maskw": maskw,
        "ident": identm,
        "validc1": validc1,
        "triu": triu_m,
        "iota_h": iota_m,
    }


def kernel(L0_attn_x, L0_attn_w_qkv, L0_attn_w_out, L0_attn_q_gamma, L0_attn_k_gamma,
           L0_ffn_Wr, L0_ffn_Wg, L0_ffn_Wu, L0_ffn_Wd, L0_ffn_Wsg, L0_ffn_Wsu,
           L0_ffn_Wsd):
    cfg = FULL
    B, N, D = 2, 8192, 2048
    n_chunks = N // cfg.T_OWN
    inp = {
        "x": np.asarray(L0_attn_x, np.float32),
        "w_qkv": np.asarray(L0_attn_w_qkv, np.float32),
        "w_out": np.asarray(L0_attn_w_out, np.float32),
        "q_gamma": np.asarray(L0_attn_q_gamma, np.float32),
        "k_gamma": np.asarray(L0_attn_k_gamma, np.float32),
        "Wr": np.asarray(L0_ffn_Wr, np.float32),
        "Wg": np.asarray(L0_ffn_Wg, np.float32),
        "Wu": np.asarray(L0_ffn_Wu, np.float32),
        "Wd": np.asarray(L0_ffn_Wd, np.float32),
        "Wsg": np.asarray(L0_ffn_Wsg, np.float32),
        "Wsu": np.asarray(L0_ffn_Wsu, np.float32),
        "Wsd": np.asarray(L0_ffn_Wsd, np.float32),
    }
    nc = _build(cfg)
    in_maps = [_host_inputs(cfg, cid, n_chunks, inp) for cid in range(8)]
    res = run_bass_kernel_spmd(nc, in_maps, core_ids=list(range(8)))
    outp = np.empty((B, N, D), np.float32)
    for cid in range(8):
        b, cq = cid // n_chunks, cid % n_chunks
        outp[b, cq * cfg.T_OWN:(cq + 1) * cfg.T_OWN, :] = res.results[cid]["out"]
    return outp


# version 13
# speedup vs baseline: 1.0404x; 1.0404x over previous
import numpy as np
import ml_dtypes
from contextlib import ExitStack

import concourse.bass as bass
import concourse.mybir as mybir
from concourse.bass_utils import run_bass_kernel_spmd

F32 = mybir.dt.float32
BF16 = mybir.dt.bfloat16
AF = mybir.ActivationFunctionType
ALU = mybir.AluOpType
AX = mybir.AxisListType
BFNP = ml_dtypes.bfloat16

NEG = -1e30
LN_EPS = 1e-5
RMS_EPS = 1e-6
THETA = 10000.0


class Cfg:
    def __init__(self, D, NH, T_OWN, W, FS, F, E=8):
        self.D, self.NH, self.T_OWN, self.W, self.FS, self.F, self.E = D, NH, T_OWN, W, FS, F, E
        assert NH * 128 == D
        assert T_OWN % 512 == 0 and W % 512 == 0 and D % 128 == 0
        self.T_EXT = W
        self.T_FRAME = W + T_OWN
        self.DC = D // 128
        self.NTT = self.T_FRAME // 128
        self.NTT_OWN = T_OWN // 128
        self.NCH = self.T_FRAME // 512
        self.WT = W // 128
        self.MT = self.WT + 4
        self.QC = T_OWN // 512
        self.NFS = FS // 128
        self.NF = F // 128
        self.CGW = min(512, D)
        self.NCG = D // self.CGW
        self.HCG = self.CGW // 128
        self.OCW = min(512, D)
        self.NOC = D // self.OCW
        self.HALF = T_OWN // 2
        assert self.HALF % 512 == 0
        self.NT5 = self.HALF // 512
        self.NTH = self.HALF // 128
        self.CAP = (3 * T_OWN) // 8
        assert self.CAP % 128 == 0
        self.NSUB = self.CAP // 128


FULL = Cfg(D=2048, NH=16, T_OWN=2048, W=4096, FS=2816, F=1408)


class Chain:
    def __init__(self, nc, ctx, name):
        self.h = ctx.enter_context(nc.semaphore(name))
        self.n = 0

    def inc(self, instr, k=1):
        instr.then_inc(self.h, k)
        self.n += k
        return self.n

    def wait(self, eng, n=None):
        eng.wait_ge(self.h, self.n if n is None else n)


def _build(cfg):
    c = cfg
    nc = bass.Bass("TRN2", target_bir_lowering=False)

    x_pad = nc.dram_tensor("x_pad", [c.T_FRAME, c.D], F32, kind="ExternalInput").ap()
    wqkv = nc.dram_tensor("wqkv", [c.D, 3 * c.D], F32, kind="ExternalInput").ap()
    wout = nc.dram_tensor("wout", [c.D, c.D], F32, kind="ExternalInput").ap()
    wr_t = nc.dram_tensor("wr_t", [c.D, c.E], F32, kind="ExternalInput").ap()
    wsg_t = nc.dram_tensor("wsg_t", [c.D, c.FS], F32, kind="ExternalInput").ap()
    wsu_t = nc.dram_tensor("wsu_t", [c.D, c.FS], F32, kind="ExternalInput").ap()
    wsd_t = nc.dram_tensor("wsd_t", [c.FS, c.D], F32, kind="ExternalInput").ap()
    wg_t = nc.dram_tensor("wg_t", [c.E, c.D, c.F], F32, kind="ExternalInput").ap()
    wu_t = nc.dram_tensor("wu_t", [c.E, c.D, c.F], F32, kind="ExternalInput").ap()
    wd_t = nc.dram_tensor("wd_t", [c.E, c.F, c.D], F32, kind="ExternalInput").ap()
    gq_row = nc.dram_tensor("gq_row", [1, 128], F32, kind="ExternalInput").ap()
    gk_row = nc.dram_tensor("gk_row", [1, 128], F32, kind="ExternalInput").ap()
    cos_t = nc.dram_tensor("cos_t", [c.T_FRAME, 128], F32, kind="ExternalInput").ap()
    sin_t = nc.dram_tensor("sin_t", [c.T_FRAME, 128], F32, kind="ExternalInput").ap()
    maskd = nc.dram_tensor("maskd", [128, 128], BF16, kind="ExternalInput").ap()
    maskw = nc.dram_tensor("maskw", [128, 128], BF16, kind="ExternalInput").ap()
    ident = nc.dram_tensor("ident", [128, 128], BF16, kind="ExternalInput").ap()
    validc1 = nc.dram_tensor("validc1", [128, c.NTT], F32, kind="ExternalInput").ap()
    triu = nc.dram_tensor("triu", [128, 128], BF16, kind="ExternalInput").ap()
    iota_h = nc.dram_tensor("iota_h", [128, c.NTT_OWN], mybir.dt.int32,
                            kind="ExternalInput").ap()

    out = nc.dram_tensor("out", [c.T_OWN, c.D], F32, kind="ExternalOutput").ap()

    xn1_d = nc.dram_tensor("xn1_d", [c.DC, 128, c.T_FRAME], BF16).ap()
    q_d = nc.dram_tensor("q_d", [c.NH, 128, c.T_OWN], BF16).ap()
    k_d = nc.dram_tensor("k_d", [c.NH, 128, c.T_FRAME], BF16).ap()
    v_d = nc.dram_tensor("v_d", [c.NTT, 128, c.NH * 129], BF16).ap()
    xp_d = nc.dram_tensor("xp_d", [c.T_OWN, c.D], F32).ap()
    xn2_d = nc.dram_tensor("xn2_d", [c.DC, 128, c.T_OWN], BF16).ap()
    xn2t_d = nc.dram_tensor("xn2t_d", [c.T_OWN, c.D], BF16).ap()
    wf_d = nc.dram_tensor("wf_d", [c.T_OWN, c.E], F32).ap()
    re_d = [nc.dram_tensor(f"re_d{i}", [c.T_OWN, c.D], F32).ap() for i in range(2)]
    idx_d = [nc.dram_tensor(f"idx_d{i}", [128, c.NSUB], mybir.dt.int32).ap()
             for i in range(2)]

    TEN, VEC, SCA, SYN, GPS = nc.tensor, nc.vector, nc.scalar, nc.sync, nc.gpsimd

    top = ExitStack()
    _uid = [0]

    def _sb(st, name, shape, dt):
        _uid[0] += 1
        return st.enter_context(nc.sbuf_tensor(f"{name}_{_uid[0]}", shape, dt))

    def _pt(st, name, shape):
        _uid[0] += 1
        return st.enter_context(nc.psum_tensor(f"{name}_{_uid[0]}", shape, F32))

    with top:
        c_mm = Chain(nc, top, "c_mm")
        c_ve = Chain(nc, top, "c_ve")
        c_se = Chain(nc, top, "c_se")
        c_gp = Chain(nc, top, "c_gp")
        s_sy = Chain(nc, top, "s_sy")
        s_sc = Chain(nc, top, "s_sc")
        s_gl = Chain(nc, top, "s_gl")
        s_out = Chain(nc, top, "s_out")
        s_st = Chain(nc, top, "s_st")
        sp = [Chain(nc, top, f"s_p{i}") for i in range(2)]
        sq = [Chain(nc, top, f"s_q{i}") for i in range(2)]
        sr = [Chain(nc, top, f"s_r{i}") for i in range(2)]

        ALL_ENG = [TEN, VEC, SCA, SYN, GPS]
        ALL_CH = [c_mm, c_ve, c_se, c_gp, s_sy, s_sc, s_gl, s_st] + sp + sq + sr

        def barrier():
            for eng in ALL_ENG:
                for ch in ALL_CH:
                    if ch.n:
                        ch.wait(eng)

        def vsync():
            VEC.wait_ge(c_ve.h, c_ve.n)

        def ssync():
            SCA.wait_ge(c_se.h, c_se.n)

        def gsync():
            GPS.wait_ge(c_gp.h, c_gp.n)

        def emit_ln(x_ap, xsq_ap, lnst, out_ap):
            c_se.inc(SCA.activation(xsq_ap, x_ap, AF.Square,
                                    accum_out=lnst[:, 0:1]))
            c_ve.inc(VEC.reduce_sum(lnst[:, 1:2], x_ap, axis=AX.X))
            c_se.wait(VEC)
            vsync()
            c_ve.inc(VEC.tensor_scalar_mul(lnst[:, 2:3], lnst[:, 1:2], 1.0 / c.D))
            c_ve.inc(VEC.tensor_scalar_mul(lnst[:, 3:4], lnst[:, 0:1], 1.0 / c.D))
            vsync()
            c_ve.inc(VEC.scalar_tensor_tensor(
                lnst[:, 4:5], lnst[:, 2:3], lnst[:, 2:3], lnst[:, 3:4],
                ALU.mult, ALU.subtract))
            c_ve.wait(SCA)
            c_se.inc(SCA.activation(lnst[:, 5:6], lnst[:, 4:5], AF.Sqrt,
                                    bias=K_LNEPS, scale=K_M1))
            c_se.wait(VEC)
            c_ve.inc(VEC.reciprocal(lnst[:, 6:7], lnst[:, 5:6]))
            vsync()
            c_ve.inc(VEC.scalar_tensor_tensor(
                lnst[:, 7:8], lnst[:, 2:3], -1.0, lnst[:, 6:7],
                ALU.mult, ALU.mult))
            vsync()
            c_ve.inc(VEC.tensor_scalar(
                out_ap, x_ap, lnst[:, 6:7], lnst[:, 7:8], ALU.mult, ALU.add))

        maskd_sb = _sb(top, "maskd_sb", [128, 128], BF16)
        maskw_sb = _sb(top, "maskw_sb", [128, 128], BF16)
        ident_sb = _sb(top, "ident_sb", [128, 128], BF16)
        validc1_sb = _sb(top, "validc1_sb", [128, c.NTT], F32)
        gq_sb = _sb(top, "gq_sb", [1, 128], F32)
        gk_sb = _sb(top, "gk_sb", [1, 128], F32)

        consts = _sb(top, "consts", [128, 8], F32)
        CONST_VALS = [LN_EPS, -1.0, 128.0 * RMS_EPS, RMS_EPS, 1.0 / 128.0, 1.0, 0.0, 0.0]
        for i, v in enumerate(CONST_VALS):
            c_ve.inc(VEC.memset(consts[:, i:i + 1], v))
        K_LNEPS = consts[:, 0:1]
        K_M1 = consts[:, 1:2]
        K_QEPS = consts[:, 2:3]
        K_KEPS = consts[:, 3:4]
        K_I128 = consts[:, 4:5]

        s_sy.inc(SYN.dma_start(maskd_sb[:], maskd[:, :]), 16)
        s_sy.inc(SYN.dma_start(maskw_sb[:], maskw[:, :]), 16)
        s_sy.inc(SYN.dma_start(ident_sb[:], ident[:, :]), 16)
        s_sy.inc(SYN.dma_start(validc1_sb[:], validc1[:, :]), 16)
        s_sy.inc(SYN.dma_start(gq_sb[:], gq_row[:, :]), 16)
        s_sy.inc(SYN.dma_start(gk_sb[:], gk_row[:, :]), 16)

        def proj_pass(kind):
            norm = kind in ("k", "q")
            colbase = {"q": 0, "k": c.D, "v": 2 * c.D}[kind]
            chunks = list(range(c.NCH)) if kind in ("k", "v") else \
                list(range(c.T_EXT // 512, c.NCH))
            with ExitStack() as ps:
                w_sb = _sb(ps, "w_sb", [128, c.DC, c.D], BF16)
                xn1fm = [_sb(ps, f"xn1fm{i}", [128, c.DC, 512], BF16) for i in range(2)]
                NBK = 6
                pp = [_pt(ps, f"pp{i}", [128, 512]) for i in range(NBK)]
                bank_se = [0] * NBK
                for cc in range(c.DC):
                    s_gl.inc(GPS.dma_start(
                        w_sb[:, cc, :],
                        wqkv[cc * 128:(cc + 1) * 128, colbase:colbase + c.D]), 16)
                wres_n = s_gl.n

                if norm:
                    g_sb = gq_sb if kind == "q" else gk_sb
                    gb = _sb(ps, "gb", [128, 128], F32)
                    pbc = _pt(ps, "pbc", [128, 128])
                    onef = _sb(ps, "onef", [1, 128], F32)
                    VEC.wait_ge(s_sy.h, s_sy.n)
                    c_ve.inc(VEC.memset(onef[:], 1.0))
                    c_ve.wait(TEN)
                    TEN.wait_ge(s_sy.h, s_sy.n)
                    c_mm.inc(TEN.matmul(pbc[:], onef[:], g_sb[:], start=True, stop=True))
                    c_mm.wait(VEC)
                    c_ve.inc(VEC.tensor_copy(gb[:], pbc[:]))
                    tabraw = [_sb(ps, f"tr{i}", [128, 2, 4, 128], F32) for i in range(2)]
                    csg = [_sb(ps, f"cs{i}", [128, 2, 4, 128], BF16) for i in range(2)]
                    kcp = [_sb(ps, f"kcp{i}", [128, c.D], BF16) for i in range(2)]
                    t1b = _sb(ps, "t1b", [128, c.D], BF16)
                    t2b = _sb(ps, "t2b", [128, c.D], BF16)
                    t12b = _sb(ps, "t12b", [128, c.D], BF16)
                    sqt = _sb(ps, "sqt", [128, c.D], BF16)
                    ssq16 = _sb(ps, "ssq16", [128, c.NH], F32)
                    s16 = _sb(ps, "s16", [128, c.NH], F32)
                    s16t = _sb(ps, "s16t", [128, c.NH], F32)
                    otile = [_sb(ps, f"ot{i}", [128, c.D], BF16) for i in range(2)]
                    fmst = [_sb(ps, f"fm{i}", [128, c.NH, 128], BF16) for i in range(2)]
                else:
                    vt = [_sb(ps, f"vt{i}", [128, c.NH, 129], BF16) for i in range(2)]

                if kind == "k":
                    x_f = [_sb(ps, f"x_f{i}", [128, c.D], F32) for i in range(4)]
                    xsq = [_sb(ps, f"xsq{i}", [128, c.D], BF16) for i in range(2)]
                    xn1t = [_sb(ps, f"xn1t{i}", [128, c.D], BF16) for i in range(2)]
                    lnB = _sb(ps, "lnB", [128, 8, 4], F32)
                    xf_free = [(0, 0)] * 4
                    xt_free = [0, 0]

                fm_cons = [0, 0]
                fmW_free = [0, 0]
                ot_free = [0, 0]
                fmst_free = [0, 0]
                tab_cons = [0, 0]
                tabraw_free = [0, 0]
                kcp_free = [0, 0]

                for chi, ch in enumerate(chunks):
                    par = chi % 2
                    if kind == "k":
                        ldn = [0] * 4
                        for tt in range(4):
                            ft = ch * 4 + tt
                            se0, ve0 = xf_free[tt]
                            SYN.wait_ge(c_se.h, se0)
                            SYN.wait_ge(c_ve.h, ve0)
                            ld = sp[tt % 2]
                            ld.inc(SYN.dma_start(
                                x_f[tt][:], x_pad[ft * 128:(ft + 1) * 128, :]), 16)
                            ldn[tt] = ld.n
                        for tt in range(4):
                            SCA.wait_ge(sp[tt % 2].h, ldn[tt])
                            c_se.inc(SCA.activation(
                                xsq[tt % 2][:], x_f[tt][:], AF.Square,
                                accum_out=lnB[:, 0, tt:tt + 1]))
                            VEC.wait_ge(sp[tt % 2].h, ldn[tt])
                            c_ve.inc(VEC.reduce_sum(
                                lnB[:, 1, tt:tt + 1], x_f[tt][:], axis=AX.X))
                        c_se.wait(VEC)
                        vsync()
                        c_ve.inc(VEC.tensor_scalar_mul(lnB[:, 2, :], lnB[:, 1, :],
                                                       1.0 / c.D))
                        c_ve.inc(VEC.tensor_scalar_mul(lnB[:, 3, :], lnB[:, 0, :],
                                                       1.0 / c.D))
                        vsync()
                        c_ve.inc(VEC.tensor_tensor(
                            lnB[:, 4, :], lnB[:, 2, :], lnB[:, 2, :], ALU.mult))
                        vsync()
                        c_ve.inc(VEC.tensor_tensor(
                            lnB[:, 4, :], lnB[:, 4, :], lnB[:, 3, :], ALU.subtract))
                        c_ve.wait(SCA)
                        c_se.inc(SCA.activation(lnB[:, 5, :], lnB[:, 4, :], AF.Sqrt,
                                                bias=K_LNEPS, scale=K_M1))
                        c_se.wait(VEC)
                        c_ve.inc(VEC.reciprocal(lnB[:, 6, :], lnB[:, 5, :]))
                        vsync()
                        c_ve.inc(VEC.tensor_tensor(
                            lnB[:, 7, :], lnB[:, 2, :], lnB[:, 6, :], ALU.mult))
                        vsync()
                        c_ve.inc(VEC.tensor_scalar_mul(lnB[:, 7, :], lnB[:, 7, :],
                                                       -1.0))
                        vsync()
                        fmT = [0] * 4
                        for tt in range(4):
                            tp = tt % 2
                            VEC.wait_ge(s_sc.h, xt_free[tp])
                            c_ve.inc(VEC.tensor_scalar(
                                xn1t[tp][:], x_f[tt][:], lnB[:, 6, tt:tt + 1],
                                lnB[:, 7, tt:tt + 1], ALU.mult, ALU.add))
                            xf_free[tt] = (c_se.n, c_ve.n)
                            SCA.wait_ge(c_ve.h, c_ve.n)
                            if tt == 0:
                                SCA.wait_ge(c_mm.h, fm_cons[par])
                                SCA.wait_ge(s_sc.h, fmW_free[par])
                            s_sc.inc(SCA.dma_start(
                                xn1fm[par][:, :, tt * 128:(tt + 1) * 128],
                                xn1t[tp][:].rearrange("p (b d) -> p b d", d=128),
                                transpose=True), 16)
                            xt_free[tp] = s_sc.n
                            fmT[tt] = s_sc.n
                        SCA.wait_ge(s_sc.h, s_sc.n)
                        s_sc.inc(SCA.dma_start(
                            xn1_d[:, :, ch * 512:(ch + 1) * 512].transpose([1, 0, 2]),
                            xn1fm[par][:, :, :]), 16)
                        fmW_free[par] = s_sc.n
                        fm_mode, fm_val = "sc", fmT
                    else:
                        SYN.wait_ge(c_mm.h, fm_cons[par])
                        ld = sq[par]
                        ld.inc(SYN.dma_start(
                            xn1fm[par][:, :, :],
                            xn1_d[:, :, ch * 512:(ch + 1) * 512].transpose([1, 0, 2])), 16)
                        fm_mode, fm_val = "ld", (ld, ld.n)

                    if norm:
                        ldt = sr[par]
                        SYN.wait_ge(c_ve.h, tabraw_free[par])
                        ldt.inc(SYN.dma_start(
                            tabraw[par][:, 0],
                            cos_t[ch * 512:(ch + 1) * 512, :].rearrange(
                                "(a p) d -> p a d", p=128)), 16)
                        ldt.inc(SYN.dma_start(
                            tabraw[par][:, 1],
                            sin_t[ch * 512:(ch + 1) * 512, :].rearrange(
                                "(a p) d -> p a d", p=128)), 16)
                        VEC.wait_ge(ldt.h, ldt.n)
                        gbb = gb[:].unsqueeze(1).broadcast_to([128, 4, 128])
                        c_ve.inc(VEC.tensor_tensor(csg[par][:, 0],
                                                   tabraw[par][:, 0], gbb, ALU.mult))
                        c_ve.inc(VEC.tensor_tensor(
                            csg[par][:, 1, :, 0:64], tabraw[par][:, 1, :, 0:64],
                            gb[:, 64:128].unsqueeze(1).broadcast_to([128, 4, 64]),
                            ALU.mult))
                        c_ve.inc(VEC.tensor_tensor(
                            csg[par][:, 1, :, 64:128], tabraw[par][:, 1, :, 64:128],
                            gb[:, 0:64].unsqueeze(1).broadcast_to([128, 4, 64]),
                            ALU.mult))
                        tab_cons[par] = c_ve.n
                        tabraw_free[par] = c_ve.n

                    for tt in range(4):
                        ft = ch * 4 + tt
                        tp = tt % 2
                        copy_ns = 0
                        for cg in range(c.NCG):
                            gi = (chi * 4 + tt) * c.NCG + cg
                            bk = gi % NBK
                            ps_t = pp[bk]
                            if fm_mode == "sc":
                                TEN.wait_ge(s_sc.h, fm_val[tt])
                            else:
                                TEN.wait_ge(fm_val[0].h, fm_val[1])
                            TEN.wait_ge(s_gl.h, wres_n)
                            TEN.wait_ge(c_se.h, bank_se[bk])
                            for cc in range(c.DC):
                                ins = TEN.matmul(
                                    ps_t[:, 0:c.CGW],
                                    xn1fm[par][:, cc, tt * 128:(tt + 1) * 128],
                                    w_sb[:, cc, cg * c.CGW:(cg + 1) * c.CGW],
                                    start=(cc == 0), stop=(cc == c.DC - 1))
                            c_mm.inc(ins)
                            fm_cons[par] = c_mm.n
                            ps3 = ps_t[:, 0:c.CGW].rearrange("p (h d) -> p h d", d=128)
                            if norm:
                                c_mm.wait(SCA)
                                SCA.wait_ge(c_ve.h, kcp_free[tp])
                                c_se.inc(SCA.copy(
                                    kcp[tp][:, cg * c.CGW:(cg + 1) * c.CGW],
                                    ps_t[:, 0:c.CGW]))
                                bank_se[bk] = c_se.n
                                copy_ns = c_se.n
                            else:
                                c_mm.wait(SCA)
                                SCA.wait_ge(s_sc.h, ot_free[tp])
                                c_se.inc(SCA.copy(
                                    vt[tp][:, cg * c.HCG:(cg + 1) * c.HCG, 0:128], ps3))
                                bank_se[bk] = c_se.n
                        if norm:
                            VEC.wait_ge(c_se.h, copy_ns)
                            kc = kcp[tp][:]
                            kc3 = kc.rearrange("p (h d) -> p h d", d=128)
                            c_ve.inc(VEC.tensor_tensor(sqt[:], kc, kc, ALU.mult))
                            vsync()
                            c_ve.inc(VEC.reduce_sum(
                                ssq16[:], sqt[:].rearrange("p (h d) -> p h d", d=128),
                                axis=AX.X))
                            c_ve.wait(SCA)
                            if kind == "q":
                                c_se.inc(SCA.activation(s16t[:], ssq16[:], AF.Sqrt,
                                                        bias=K_QEPS))
                            else:
                                c_se.inc(SCA.activation(s16t[:], ssq16[:], AF.Sqrt,
                                                        bias=K_KEPS, scale=K_I128))
                            c_se.wait(VEC)
                            c_ve.inc(VEC.reciprocal(s16[:], s16t[:]))
                            VEC.wait_ge(c_ve.h, tab_cons[par])
                            cg3 = csg[par][:, 0, tt, :].unsqueeze(1).broadcast_to(
                                [128, c.NH, 128])
                            t1v = t1b[:].rearrange("p (h d) -> p h d", d=128)
                            t2v = t2b[:].rearrange("p (h d) -> p h d", d=128)
                            c_ve.inc(VEC.tensor_tensor(t1v, kc3, cg3, ALU.mult))
                            sg_lo = csg[par][:, 1, tt, 0:64].unsqueeze(1).broadcast_to(
                                [128, c.NH, 64])
                            sg_hi = csg[par][:, 1, tt, 64:128].unsqueeze(1).broadcast_to(
                                [128, c.NH, 64])
                            c_ve.inc(VEC.tensor_tensor(
                                t2v[:, :, 0:64], kc3[:, :, 64:128], sg_lo, ALU.mult))
                            c_ve.inc(VEC.tensor_tensor(
                                t2v[:, :, 64:128], kc3[:, :, 0:64], sg_hi, ALU.mult))
                            kcp_free[tp] = c_ve.n
                            vsync()
                            c_ve.inc(VEC.tensor_tensor(t12b[:], t1b[:], t2b[:],
                                                       ALU.add))
                            vsync()
                            VEC.wait_ge(s_sc.h, ot_free[tp])
                            for hh in range(c.NH):
                                c_ve.inc(VEC.tensor_scalar(
                                    otile[tp][:, hh * 128:(hh + 1) * 128],
                                    t12b[:, hh * 128:(hh + 1) * 128],
                                    s16[:, hh:hh + 1], None, ALU.mult))
                            SCA.wait_ge(c_ve.h, c_ve.n)
                            SCA.wait_ge(s_sc.h, fmst_free[tp])
                            s_sc.inc(SCA.dma_start(
                                fmst[tp][:, :, :],
                                otile[tp][:].rearrange("p (b d) -> p b d", d=128),
                                transpose=True), 16)
                            ot_free[tp] = s_sc.n
                            SCA.wait_ge(s_sc.h, s_sc.n)
                            dst = k_d if kind == "k" else q_d
                            col0 = ft * 128 if kind == "k" else ft * 128 - c.T_EXT
                            s_sc.inc(SCA.dma_start(
                                dst[:, :, col0:col0 + 128].transpose([1, 0, 2]),
                                fmst[tp][:, :, :]), 16)
                            fmst_free[tp] = s_sc.n
                        else:
                            c_se.wait(VEC)
                            VEC.wait_ge(s_sy.h, s_sy.n)
                            VEC.wait_ge(s_sc.h, ot_free[tp])
                            c_ve.inc(VEC.tensor_copy(
                                vt[tp][:, :, 128:129],
                                validc1_sb[:, ft:ft + 1].unsqueeze(1).broadcast_to(
                                    [128, c.NH, 1])))
                            SCA.wait_ge(c_ve.h, c_ve.n)
                            SCA.wait_ge(c_se.h, c_se.n)
                            s_sc.inc(SCA.dma_start(v_d[ft, :, :], vt[tp][:, :, :]), 16)
                            ot_free[tp] = s_sc.n

        proj_pass("k")
        barrier()
        proj_pass("q")
        barrier()
        proj_pass("v")
        barrier()

        ps23 = ExitStack()
        O_sb = _sb(ps23, "O_sb", [128, c.NTT_OWN, c.NH, 128], BF16)
        with ExitStack() as ps:
            kwin = [_sb(ps, f"kwin{i}", [128, c.W + 512], BF16) for i in range(2)]
            vwin = [_sb(ps, f"vwin{i}", [128, c.MT, 129], BF16) for i in range(2)]
            qt = [_sb(ps, f"qt{i}", [128, 512], BF16) for i in range(2)]
            NSP = 4
            pT = [_sb(ps, f"pT{i}", [128, 512], BF16) for i in range(NSP)]
            rl_sb = _sb(ps, "rl_sb", [128, 1], F32)
            SP = [_pt(ps, f"SP{i}", [128, 512])
                  for i in range(NSP)]
            OP = [_pt(ps, f"OP{i}", [128, 129])
                  for i in range(4)]

            NIT = c.QC * c.NH
            win_cons = [0, 0]
            sp_free = [0] * NSP
            pt_free = [0] * NSP
            op_free = [0] * 4

            def issue_loads(it):
                qc, h = divmod(it, c.NH)
                pi = it % 2
                SYN.wait_ge(c_mm.h, win_cons[pi])
                ld = sp[pi]
                ld.inc(SYN.dma_start(
                    kwin[pi][:], k_d[h, :, qc * 512:qc * 512 + c.W + 512]), 16)
                ld.inc(SYN.dma_start(
                    vwin[pi][:],
                    v_d[qc * 4:qc * 4 + c.MT, :, h * 129:(h + 1) * 129].transpose(
                        [1, 0, 2])), 16)
                ld.inc(SYN.dma_start(qt[pi][:], q_d[h, :, qc * 512:(qc + 1) * 512]), 16)
                return ld.n

            load_n = [0] * NIT
            load_n[0] = issue_loads(0)
            if NIT > 1:
                load_n[1] = issue_loads(1)

            for it in range(NIT):
                qc, h = divmod(it, c.NH)
                pi = it % 2
                TEN.wait_ge(sp[pi].h, load_n[it])
                exp_ns = {}

                def emit_S(m):
                    bk = m % NSP
                    jlo = max(0, m - c.WT)
                    jhi = min(3, m)
                    has_dmask = 0 <= m - c.WT <= 3
                    has_wmask = m <= 3
                    TEN.wait_ge(c_se.h, sp_free[bk])
                    ins = TEN.matmul(
                        SP[bk][:, jlo * 128:(jhi + 1) * 128],
                        kwin[pi][:, m * 128:(m + 1) * 128],
                        qt[pi][:, jlo * 128:(jhi + 1) * 128],
                        start=True, stop=not (has_dmask or has_wmask))
                    if has_dmask:
                        j = m - c.WT
                        ins = TEN.matmul(
                            SP[bk][:, j * 128:(j + 1) * 128],
                            ident_sb[:], maskd_sb[:], start=False, stop=True)
                    if has_wmask:
                        j = m
                        ins = TEN.matmul(
                            SP[bk][:, j * 128:(j + 1) * 128],
                            ident_sb[:], maskw_sb[:], start=False, stop=True)
                    c_mm.inc(ins)
                    c_mm.wait(SCA)
                    SCA.wait_ge(c_mm.h, pt_free[bk])
                    c_se.inc(SCA.activation(
                        pT[bk][:, jlo * 128:(jhi + 1) * 128],
                        SP[bk][:, jlo * 128:(jhi + 1) * 128], AF.Exp))
                    exp_ns[m] = c_se.n
                    sp_free[bk] = c_se.n

                def emit_AV(m):
                    bk = m % NSP
                    jlo = max(0, m - c.WT)
                    jhi = min(3, m)
                    TEN.wait_ge(c_se.h, exp_ns[m])
                    for j in range(jlo, jhi + 1):
                        if m == j:
                            TEN.wait_ge(c_ve.h, op_free[j])
                        ins = TEN.matmul(
                            OP[j][:],
                            pT[bk][:, j * 128:(j + 1) * 128],
                            vwin[pi][:, m, :],
                            start=(m == j), stop=(m == j + c.WT))
                    c_mm.inc(ins)
                    pt_free[bk] = c_mm.n

                for m in range(c.MT):
                    emit_S(m)
                    if m >= 2:
                        emit_AV(m - 2)
                emit_AV(c.MT - 2)
                emit_AV(c.MT - 1)
                win_cons[pi] = c_mm.n
                for j in range(4):
                    c_mm.wait(VEC)
                    c_ve.inc(VEC.reciprocal(rl_sb[:], OP[j][:, 128:129]))
                    vsync()
                    c_ve.inc(VEC.tensor_scalar(
                        O_sb[:, qc * 4 + j, h, :], OP[j][:, 0:128], rl_sb[:, 0:1],
                        None, ALU.mult))
                    op_free[j] = c_ve.n
                if it + 2 < NIT:
                    load_n[it + 2] = issue_loads(it + 2)

        barrier()

        with ExitStack() as ps:
            wout_sb = _sb(ps, "wout_sb", [128, c.DC, c.D], BF16)
            ofm = [_sb(ps, f"ofm{i}", [128, c.NH, 128], BF16) for i in range(2)]
            x_f = [_sb(ps, f"x3f{i}", [128, c.D], F32) for i in range(2)]
            xp_sb = [_sb(ps, f"xp{i}", [128, c.D], F32) for i in range(2)]
            xsq3 = _sb(ps, "xsq3", [128, c.D], BF16)
            xn2t = [_sb(ps, f"xn2t{i}", [128, c.D], BF16) for i in range(2)]
            fm2 = [_sb(ps, f"fm2_{i}", [128, c.DC, 128], BF16) for i in range(2)]
            lnst3 = _sb(ps, "lnst3", [128, 8], F32)
            pp3 = [_pt(ps, f"pp3{i}", [128, 512])
                   for i in range(4)]
            for cc in range(c.DC):
                s_gl.inc(GPS.dma_start(wout_sb[:, cc, :],
                                       wout[cc * 128:(cc + 1) * 128, :]), 16)
            wres_n = s_gl.n

            ofm_free = [0, 0]
            xf_free = [0, 0]
            xp_free = [0, 0]
            fmt_prev = [0, 0]
            bank_ve3 = [0, 0, 0, 0]

            for tt in range(c.NTT_OWN):
                tp = tt % 2
                SCA.wait_ge(c_mm.h, ofm_free[tp])
                s_sc.inc(SCA.dma_start(ofm[tp][:, :, :], O_sb[:, tt, :, :],
                                       transpose=True), 16)
                ofm_n = s_sc.n
                SYN.wait_ge(c_ve.h, xf_free[tp])
                ld = sp[tp]
                ld.inc(SYN.dma_start(
                    x_f[tp][:],
                    x_pad[c.T_EXT + tt * 128:c.T_EXT + (tt + 1) * 128, :]), 16)
                for oc in range(c.NOC):
                    bk = (tt * c.NOC + oc) % 4
                    TEN.wait_ge(s_sc.h, ofm_n)
                    TEN.wait_ge(s_gl.h, wres_n)
                    TEN.wait_ge(c_ve.h, bank_ve3[bk])
                    for cc in range(c.DC):
                        ins = TEN.matmul(
                            pp3[bk][:, 0:c.OCW], ofm[tp][:, cc, :],
                            wout_sb[:, cc, oc * c.OCW:(oc + 1) * c.OCW],
                            start=(cc == 0), stop=(cc == c.DC - 1))
                    c_mm.inc(ins)
                    ofm_free[tp] = c_mm.n
                    c_mm.wait(VEC)
                    VEC.wait_ge(ld.h, ld.n)
                    VEC.wait_ge(s_st.h, xp_free[tp])
                    VEC.wait_ge(s_sc.h, fmt_prev[tp])
                    c_ve.inc(VEC.tensor_tensor(
                        xp_sb[tp][:, oc * c.OCW:(oc + 1) * c.OCW], pp3[bk][:, 0:c.OCW],
                        x_f[tp][:, oc * c.OCW:(oc + 1) * c.OCW], ALU.add))
                    bank_ve3[bk] = c_ve.n
                xf_free[tp] = c_ve.n
                c_ve.wait(SCA)
                vsync()
                SCA.wait_ge(s_st.h, xp_free[tp])
                SCA.wait_ge(s_sc.h, fmt_prev[tp])
                emit_ln(xp_sb[tp][:], xsq3[:], lnst3, xn2t[tp][:])
                SCA.wait_ge(c_ve.h, c_ve.n)
                s_sc.inc(SCA.dma_start(
                    fm2[tp][:, :, :],
                    xn2t[tp][:].rearrange("p (b d) -> p b d", d=128),
                    transpose=True), 16)
                fmt_prev[tp] = s_sc.n
                SYN.wait_ge(c_ve.h, c_ve.n)
                SYN.wait_ge(c_se.h, c_se.n)
                SYN.wait_ge(s_st.h, xp_free[tp])
                s_st.inc(SYN.dma_start(xp_d[tt * 128:(tt + 1) * 128, :], xp_sb[tp][:]), 16)
                s_st.inc(SYN.dma_start(xn2t_d[tt * 128:(tt + 1) * 128, :], xn2t[tp][:]), 16)
                SYN.wait_ge(s_sc.h, fmt_prev[tp])
                s_st.inc(SYN.dma_start(
                    xn2_d[:, :, tt * 128:(tt + 1) * 128].transpose([1, 0, 2]),
                    fm2[tp][:, :, :]), 16)
                xp_free[tp] = s_st.n

        ps23.close()
        barrier()

        I32 = mybir.dt.int32
        IOA = bass.IndirectOffsetOnAxis
        with ExitStack() as ps:
            wr_sb = _sb(ps, "wr_sb", [128, c.DC, c.E], BF16)
            triu_sb = _sb(ps, "triu_sb", [128, 128], BF16)
            iota_sb = _sb(ps, "iota_sb", [128, c.NTT_OWN], I32)
            wf_res = _sb(ps, "wf_res", [128, c.NTT_OWN, c.E], F32)
            for cc in range(c.DC):
                s_gl.inc(GPS.dma_start(wr_sb[:, cc, :],
                                       wr_t[cc * 128:(cc + 1) * 128, :]), 16)
            s_gl.inc(GPS.dma_start(triu_sb[:], triu[:, :]), 16)
            s_gl.inc(GPS.dma_start(iota_sb[:], iota_h[:, :]), 16)
            wr_n = s_gl.n
            with ExitStack() as zs:
                zer_sb = _sb(zs, "zer_sb", [128, c.D], F32)
                c_ve.inc(VEC.memset(zer_sb[:], 0.0))
                SCA.wait_ge(c_ve.h, c_ve.n)
                for tl in range(c.NTT_OWN):
                    for rb in range(2):
                        s_sc.inc(SCA.dma_start(
                            re_d[rb][tl * 128:(tl + 1) * 128, 0:c.D],
                            zer_sb[:, 0:c.D]), 16)
                re_zero_n = s_sc.n
                for _eng in ALL_ENG:
                    _eng.wait_ge(s_sc.h, re_zero_n)

            with ExitStack() as rs:
                xnc = [_sb(rs, f"xnc{i}", [128, c.DC, 512], BF16) for i in range(2)]
                sm_sb = _sb(rs, "sm_sb", [128, c.E], F32)
                st_sb = _sb(rs, "st_sb", [128, 6], F32)
                pr = _pt(rs, "pr", [128, c.E])
                xnc_cons = [0, 0]
                pr_free = (0, 0)
                for chq in range(c.T_OWN // 512):
                    par = chq % 2
                    SYN.wait_ge(c_mm.h, xnc_cons[par])
                    ld = sq[par]
                    ld.inc(SYN.dma_start(
                        xnc[par][:, :, :],
                        xn2_d[:, :, chq * 512:(chq + 1) * 512].transpose([1, 0, 2])), 16)
                    for st4 in range(4):
                        tl = chq * 4 + st4
                        TEN.wait_ge(ld.h, ld.n)
                        TEN.wait_ge(s_gl.h, wr_n)
                        TEN.wait_ge(c_ve.h, pr_free[0])
                        TEN.wait_ge(c_se.h, pr_free[1])
                        for cc in range(c.DC):
                            ins = TEN.matmul(
                                pr[:], xnc[par][:, cc, st4 * 128:(st4 + 1) * 128],
                                wr_sb[:, cc, :], start=(cc == 0),
                                stop=(cc == c.DC - 1))
                        c_mm.inc(ins)
                        xnc_cons[par] = c_mm.n
                        c_mm.wait(VEC)
                        c_ve.inc(VEC.reduce_max(st_sb[:, 0:1], pr[:], axis=AX.X,
                                                negate=True))
                        c_ve.wait(SCA)
                        c_se.inc(SCA.activation(sm_sb[:], pr[:], AF.Exp,
                                                bias=st_sb[:, 0:1]))
                        pr_free = (c_ve.n, c_se.n)
                        c_se.wait(VEC)
                        c_ve.inc(VEC.reduce_sum(st_sb[:, 1:2], sm_sb[:], axis=AX.X))
                        vsync()
                        c_ve.inc(VEC.reciprocal(st_sb[:, 2:3], st_sb[:, 1:2]))
                        vsync()
                        c_ve.inc(VEC.tensor_scalar(sm_sb[:], sm_sb[:], st_sb[:, 2:3],
                                                   None, ALU.mult))
                        vsync()
                        c_ve.inc(VEC.reduce_max(st_sb[:, 3:4], sm_sb[:], axis=AX.X,
                                                negate=True))
                        vsync()
                        c_ve.inc(VEC.tensor_scalar(wf_res[:, tl, :], sm_sb[:],
                                                   st_sb[:, 3:4], 0.0, ALU.add,
                                                   ALU.is_ge))
                        vsync()
                        c_ve.inc(VEC.scalar_tensor_tensor(
                            wf_res[:, tl, :], wf_res[:, tl, :], NEG, sm_sb[:],
                            ALU.mult, ALU.add))
                        vsync()
                        c_ve.inc(VEC.reduce_max(st_sb[:, 4:5], wf_res[:, tl, :],
                                                axis=AX.X, negate=True))
                        vsync()
                        c_ve.inc(VEC.tensor_scalar(wf_res[:, tl, :], sm_sb[:],
                                                   st_sb[:, 4:5], 0.0, ALU.add,
                                                   ALU.is_ge))
                        vsync()
                        c_ve.inc(VEC.tensor_tensor(wf_res[:, tl, :], wf_res[:, tl, :],
                                                   sm_sb[:], ALU.mult))
                SCA.wait_ge(c_ve.h, c_ve.n)
                s_sc.inc(SCA.dma_start(
                    wf_d[:, :].rearrange("(a p) e -> p a e", p=128), wf_res[:]), 16)
                wf_store_n = s_sc.n

            with ExitStack() as es:
                NSUB = c.NSUB
                xe = _sb(es, "xe", [128, NSUB, c.D], BF16)
                xefm = _sb(es, "xefm", [128, c.DC, c.CAP], BF16)
                geu = _sb(es, "geu", [128, c.NF, c.CAP], BF16)
                gs_sb = [_sb(es, f"gs4{i}", [128, 512], BF16) for i in range(2)]
                wa_sb = [_sb(es, f"wa4{i}", [128, c.DC, 256], BF16) for i in range(3)]
                wbe = [_sb(es, f"wbe4{i}", [128, c.NF, c.D], BF16) for i in range(2)]
                hsc = [_sb(es, f"hsc{i}", [128, c.D], F32) for i in range(2)]
                idx_sb = [_sb(es, f"idx_sb{i}", [128, NSUB], I32) for i in range(2)]
                wfe = [_sb(es, f"wfe{i}", [128, NSUB, c.E], F32) for i in range(2)]
                oob_sb = _sb(es, "oob_sb", [128, NSUB], I32)
                msk = _sb(es, "msk", [128, c.NTT_OWN], F32)
                csA = _sb(es, "csA", [128, c.NTT_OWN], F32)
                csB = _sb(es, "csB", [128, c.NTT_OWN], F32)
                rsum = _sb(es, "rsum", [128, 1], F32)
                rsum_bf = _sb(es, "rsum_bf", [128, 1], BF16)
                posb = [_sb(es, f"posb{i}", [128, c.NTT_OWN], F32) for i in range(2)]
                posi = [_sb(es, f"posi{i}", [128, c.NTT_OWN], I32) for i in range(2)]
                pg = [_pt(es, f"pg4{i}", [128, 512]) for i in range(2)]
                pu = [_pt(es, f"pu4{i}", [128, 512]) for i in range(2)]
                pa = [_pt(es, f"pa4{i}", [128, 512]) for i in range(2)]
                pcs = _pt(es, "pcs", [128, 1])

                c_ve.inc(VEC.memset(oob_sb[:], 1 << 30))
                c_ve.inc(VEC.memset(xe[:], 0.0))
                rg_cap = es.enter_context(GPS.register("rg_cap"))
                rg_town = es.enter_context(GPS.register("rg_town"))
                GPS.reg_mov(rg_cap, c.CAP)
                GPS.reg_mov(rg_town, c.T_OWN)

                t5s = []
                off = 0
                while off < c.CAP:
                    t5s.append((off, min(512, c.CAP - off)))
                    off += 512

                geu_cons = [0]
                prep_done = [0, 0]
                xefm_cons = [0, 0]
                wfe_cons = [0, 0]
                xe_free = [0]
                pcs_free = [0]
                scat_n = [0]
                scat_pp = [0, 0]
                wa_cons = [0, 0, 0]
                wbe_cons = [0, 0]
                pg_free = [0, 0]
                pu_free = [0, 0]
                pa_free = [0, 0]

                def prep(e):
                    ep = e % 2
                    VEC.wait_ge(s_sc.h, wf_store_n)
                    VEC.wait_ge(sr[ep].h, prep_done[ep])
                    c_ve.inc(VEC.tensor_scalar(
                        msk[:], wf_res[:, :, e], 0.0, None, ALU.is_gt))
                    vsync()
                    c_ve.inc(VEC.reduce_sum(rsum[:], msk[:], axis=AX.X))
                    vsync()
                    c_ve.inc(VEC.tensor_copy(rsum_bf[:], rsum[:]))
                    c_ve.wait(TEN)
                    TEN.wait_ge(s_gl.h, wr_n)
                    TEN.wait_ge(c_ve.h, pcs_free[0])
                    c_mm.inc(TEN.matmul(pcs[:], triu_sb[:], rsum_bf[:],
                                        start=True, stop=True))
                    c_mm.wait(VEC)
                    c_ve.inc(VEC.tensor_copy(csA[:], msk[:]))
                    sh = 1
                    cur, nxt = csA, csB
                    while sh < c.NTT_OWN:
                        vsync()
                        c_ve.inc(VEC.tensor_copy(nxt[:, 0:sh], cur[:, 0:sh]))
                        c_ve.inc(VEC.tensor_tensor(
                            nxt[:, sh:c.NTT_OWN], cur[:, 0:c.NTT_OWN - sh],
                            cur[:, sh:c.NTT_OWN], ALU.add))
                        cur, nxt = nxt, cur
                        sh *= 2
                    vsync()
                    c_ve.inc(VEC.tensor_tensor(cur[:], cur[:], msk[:], ALU.subtract))
                    vsync()
                    c_ve.inc(VEC.scalar_tensor_tensor(
                        posb[ep][:], rsum[:].broadcast_to([128, c.NTT_OWN]), -1.0,
                        pcs[:].broadcast_to([128, c.NTT_OWN]), ALU.mult, ALU.add))
                    pcs_free[0] = c_ve.n
                    vsync()
                    c_ve.inc(VEC.tensor_tensor(posb[ep][:], posb[ep][:], cur[:],
                                               ALU.add))
                    vsync()
                    c_ve.inc(VEC.tensor_scalar(csB[:], msk[:], -1e9, 1e9,
                                               ALU.mult, ALU.add))
                    vsync()
                    c_ve.inc(VEC.tensor_tensor(posb[ep][:], posb[ep][:], csB[:],
                                               ALU.add))
                    vsync()
                    c_ve.inc(VEC.tensor_copy(posi[ep][:], posb[ep][:]))
                    pos_ready = c_ve.n
                    lw = sr[ep]
                    GPS.wait_ge(lw.h, prep_done[ep])
                    lw.inc(GPS.dma_start(idx_d[ep][:, :], oob_sb[:]), 16)
                    GPS.wait_ge(lw.h, lw.n)
                    GPS.wait_ge(c_ve.h, pos_ready)
                    odst = idx_d[ep].rearrange("p a -> (p a)").unsqueeze(1)
                    for tl in range(c.NTT_OWN):
                        lw.inc(GPS.indirect_dma_start(
                            odst, IOA(posi[ep][:, tl:tl + 1], 0),
                            iota_sb[:, tl:tl + 1], None,
                            bounds_check=rg_cap, oob_is_err=False), 16)
                    GPS.wait_ge(lw.h, lw.n)
                    GPS.wait_ge(s_gl.h, scat_n[0])
                    lw.inc(GPS.dma_start(idx_sb[ep][:, :], idx_d[ep][:, :]), 16)
                    GPS.wait_ge(lw.h, lw.n)
                    GPS.wait_ge(s_sc.h, xe_free[0])
                    GPS.wait_ge(c_ve.h, wfe_cons[ep])
                    for i in range(NSUB):
                        lw.inc(GPS.indirect_dma_start(
                            xe[:, i, :], None, xn2t_d[:, :],
                            IOA(idx_sb[ep][:, i:i + 1], 0),
                            bounds_check=rg_town, oob_is_err=False), 16)
                        lw.inc(GPS.indirect_dma_start(
                            wfe[ep][:, i, :], None, wf_d[:, :],
                            IOA(idx_sb[ep][:, i:i + 1], 0),
                            bounds_check=rg_town, oob_is_err=False), 16)
                    gather_n = lw.n
                    prep_done[ep] = lw.n
                    return gather_n, lw.n

                def prep_b(e, gather_n):
                    ep = e % 2
                    SCA.wait_ge(sr[ep].h, gather_n)
                    SCA.wait_ge(c_mm.h, xefm_cons[0])
                    for i in range(NSUB):
                        s_sc.inc(SCA.dma_start(
                            xefm[:, :, i * 128:(i + 1) * 128],
                            xe[:, i, :].rearrange("p (b d) -> p b d", d=128),
                            transpose=True), 16)
                    xe_free[0] = s_sc.n
                    return s_sc.n

                def compute(e, xf_n, prep_lw, nxt_gather):
                    ep = e % 2
                    wp = e % 2
                    GPS.wait_ge(c_mm.h, wbe_cons[wp])
                    lwb = sq[wp]
                    lwb.inc(GPS.dma_start(
                        wbe[wp][:, :, :],
                        wd_t[e].rearrange("(ftt p) d -> p ftt d", p=128)), 16)
                    wbe_n = lwb.n
                    for ft in range(c.NF):
                        fp = ft % 3
                        GPS.wait_ge(c_mm.h, wa_cons[fp])
                        lw = sr[ft % 2]
                        lw.inc(GPS.dma_start(
                            wa_sb[fp][:, :, 0:128],
                            wg_t[e][:, ft * 128:(ft + 1) * 128].rearrange(
                                "(cc p) f -> p cc f", p=128)), 16)
                        lw.inc(GPS.dma_start(
                            wa_sb[fp][:, :, 128:256],
                            wu_t[e][:, ft * 128:(ft + 1) * 128].rearrange(
                                "(cc p) f -> p cc f", p=128)), 16)
                        my_n = lw.n
                        for t0_, tw in t5s:
                            bk = (t0_ // 512) % 2
                            TEN.wait_ge(lw.h, my_n)
                            TEN.wait_ge(s_sc.h, xf_n)
                            TEN.wait_ge(c_se.h, pg_free[bk])
                            TEN.wait_ge(c_ve.h, pu_free[bk])
                            for cc in range(c.DC):
                                TEN.matmul(
                                    pg[bk][:, 0:tw], wa_sb[fp][:, cc, 0:128],
                                    xefm[:, cc, t0_:t0_ + tw],
                                    start=(cc == 0), stop=(cc == c.DC - 1))
                            for cc in range(c.DC):
                                ins = TEN.matmul(
                                    pu[bk][:, 0:tw], wa_sb[fp][:, cc, 128:256],
                                    xefm[:, cc, t0_:t0_ + tw],
                                    start=(cc == 0), stop=(cc == c.DC - 1))
                            c_mm.inc(ins)
                            wa_cons[fp] = c_mm.n
                            xefm_cons[0] = c_mm.n
                            c_mm.wait(SCA)
                            c_se.inc(SCA.activation(gs_sb[bk][:, 0:tw],
                                                    pg[bk][:, 0:tw], AF.Silu))
                            pg_free[bk] = c_se.n
                            c_se.wait(VEC)
                            VEC.wait_ge(c_mm.h, geu_cons[0])
                            c_ve.inc(VEC.tensor_tensor(
                                geu[:, ft, t0_:t0_ + tw], gs_sb[bk][:, 0:tw],
                                pu[bk][:, 0:tw], ALU.mult))
                            pu_free[bk] = c_ve.n
                    up_done = c_ve.n
                    if nxt_gather is not None:
                        nxt_xf = prep_b(e + 1, nxt_gather)
                    else:
                        nxt_xf = 0
                    for i in range(NSUB):
                        hp = i % 2
                        for oc in range(c.NOC):
                            bk = oc % 2
                            TEN.wait_ge(lwb.h, wbe_n)
                            TEN.wait_ge(c_ve.h, max(up_done, pa_free[bk]))
                            for ftt in range(c.NF):
                                ins = TEN.matmul(
                                    pa[bk][:, 0:c.OCW],
                                    geu[:, ftt, i * 128:(i + 1) * 128],
                                    wbe[wp][:, ftt, oc * c.OCW:(oc + 1) * c.OCW],
                                    start=(ftt == 0), stop=(ftt == c.NF - 1))
                            c_mm.inc(ins)
                            wbe_cons[wp] = c_mm.n
                            geu_cons[0] = c_mm.n
                            c_mm.wait(VEC)
                            VEC.wait_ge(sr[ep].h, prep_lw)
                            VEC.wait_ge(s_gl.h, scat_n[0])
                            c_ve.inc(VEC.tensor_scalar(
                                hsc[hp][:, oc * c.OCW:(oc + 1) * c.OCW],
                                pa[bk][:, 0:c.OCW], wfe[ep][:, i, e:e + 1], None,
                                ALU.mult))
                            pa_free[bk] = c_ve.n
                        wfe_cons[ep] = c_ve.n
                        GPS.wait_ge(c_ve.h, c_ve.n)
                        GPS.wait_ge(s_sc.h, re_zero_n)
                        GPS.wait_ge(s_gl.h, scat_pp[ep])
                        s_gl.inc(GPS.indirect_dma_start(
                            re_d[ep][:, :], IOA(idx_sb[ep][:, i:i + 1], 0),
                            hsc[hp][:, :], None,
                            bounds_check=rg_town, oob_is_err=False,
                            compute_op=ALU.add), 16)
                        scat_n[0] = s_gl.n
                    scat_pp[ep] = s_gl.n
                    return nxt_xf

                SERIAL = False
                if SERIAL:
                    E_RUN = getattr(cfg, "e_run", c.E)
                    for e in range(E_RUN):
                        pn = prep(e)
                        barrier()
                        xf_n = prep_b(e, pn[0])
                        barrier()
                        compute(e, xf_n, pn[1], None)
                        barrier()
                else:
                    pn = prep(0)
                    xf_n = prep_b(0, pn[0])
                    for e in range(c.E):
                        nxt = prep(e + 1) if e + 1 < c.E else None
                        xf_n = compute(e, xf_n, pn[1],
                                       nxt[0] if nxt is not None else None)
                        pn = nxt if nxt is not None else pn

            barrier()

            for half in range(2):
                t0 = half * c.HALF
                with ExitStack() as hs:
                    xn2h = _sb(hs, "xn2h", [128, c.DC, c.HALF], BF16)
                    sgu = _sb(hs, "sgu", [128, c.NFS, c.HALF], BF16)
                    gs_sb = [_sb(hs, f"gs{i}", [128, 512], BF16) for i in range(2)]
                    wa_sb = [_sb(hs, f"wa{i}", [128, c.DC, 256], BF16)
                             for i in range(2)]
                    wbs = [_sb(hs, f"wbs{i}", [128, c.NFS, c.OCW], BF16)
                           for i in range(2)]
                    xr_t = [_sb(hs, f"xr{i}", [128, c.D], F32) for i in range(2)]
                    re_t = [_sb(hs, f"re{i}", [128, c.D], F32) for i in range(2)]
                    out_t = [_sb(hs, f"out{i}", [128, c.D], F32) for i in range(2)]
                    pg = [_pt(hs, f"pg{i}", [128, 512]) for i in range(2)]
                    pu = [_pt(hs, f"pu{i}", [128, 512]) for i in range(2)]
                    pa = [_pt(hs, f"pa{i}", [128, 512]) for i in range(2)]

                    ldh = sq[half % 2]
                    SYN.wait_ge(c_mm.h, c_mm.n)
                    ldh.inc(SYN.dma_start(
                        xn2h[:, :, :],
                        xn2_d[:, :, t0:t0 + c.HALF].transpose([1, 0, 2])), 16)
                    ldh_n = ldh.n

                    wa_cons = [0, 0]
                    wbs_cons = [0, 0]
                    pg_free = [0, 0]
                    pu_free = [0, 0]
                    pa_free = [0, 0]
                    sgu_cons = 0

                    for ft in range(c.NFS):
                        fp = ft % 2
                        GPS.wait_ge(c_mm.h, wa_cons[fp])
                        lw = sr[fp]
                        lw.inc(GPS.dma_start(
                            wa_sb[fp][:, :, 0:128],
                            wsg_t[:, ft * 128:(ft + 1) * 128].rearrange(
                                "(cc p) f -> p cc f", p=128)), 16)
                        lw.inc(GPS.dma_start(
                            wa_sb[fp][:, :, 128:256],
                            wsu_t[:, ft * 128:(ft + 1) * 128].rearrange(
                                "(cc p) f -> p cc f", p=128)), 16)
                        my_n = lw.n
                        for t5 in range(c.NT5):
                            bk = t5 % 2
                            TEN.wait_ge(lw.h, my_n)
                            TEN.wait_ge(ldh.h, ldh_n)
                            TEN.wait_ge(c_se.h, pg_free[bk])
                            TEN.wait_ge(c_ve.h, pu_free[bk])
                            for cc in range(c.DC):
                                TEN.matmul(
                                    pg[bk][:], wa_sb[fp][:, cc, 0:128],
                                    xn2h[:, cc, t5 * 512:(t5 + 1) * 512],
                                    start=(cc == 0), stop=(cc == c.DC - 1))
                            for cc in range(c.DC):
                                ins = TEN.matmul(
                                    pu[bk][:], wa_sb[fp][:, cc, 128:256],
                                    xn2h[:, cc, t5 * 512:(t5 + 1) * 512],
                                    start=(cc == 0), stop=(cc == c.DC - 1))
                            c_mm.inc(ins)
                            wa_cons[fp] = c_mm.n
                            c_mm.wait(SCA)
                            c_se.inc(SCA.activation(gs_sb[bk][:], pg[bk][:], AF.Silu))
                            pg_free[bk] = c_se.n
                            c_se.wait(VEC)
                            c_ve.inc(VEC.tensor_tensor(
                                sgu[:, ft, t5 * 512:(t5 + 1) * 512], gs_sb[bk][:],
                                pu[bk][:], ALU.mult))
                            pu_free[bk] = c_ve.n
                    up_done = c_ve.n

                    xrre_free = [0, 0]
                    out_free = [0, 0]
                    if True:
                        xr_free = [0, 0]
                        for tl in range(c.NTH):
                            tpp = tl % 2
                            row0 = t0 + tl * 128
                            SYN.wait_ge(s_sc.h, xr_free[tpp])
                            ldx = sp[tpp]
                            ldx.inc(SYN.dma_start(
                                xr_t[tpp][:], xp_d[row0:row0 + 128, :]), 16)
                            ldx.inc(SYN.dma_start(
                                re_t[tpp][:], re_d[0][row0:row0 + 128, :]), 16)
                            VEC.wait_ge(ldx.h, ldx.n)
                            c_ve.inc(VEC.tensor_tensor(
                                xr_t[tpp][:], xr_t[tpp][:], re_t[tpp][:], ALU.add))
                            SYN.wait_ge(c_ve.h, c_ve.n)
                            ldx.inc(SYN.dma_start(
                                re_t[tpp][:], re_d[1][row0:row0 + 128, :]), 16)
                            VEC.wait_ge(ldx.h, ldx.n)
                            c_ve.inc(VEC.tensor_tensor(
                                xr_t[tpp][:], xr_t[tpp][:], re_t[tpp][:], ALU.add))
                            SCA.wait_ge(c_ve.h, c_ve.n)
                            s_sc.inc(SCA.dma_start(
                                re_d[0][row0:row0 + 128, :], xr_t[tpp][:]), 16)
                            xr_free[tpp] = s_sc.n
                        xr_done = s_sc.n

                    for oc in range(c.NOC):
                        ow = oc % 2
                        GPS.wait_ge(c_mm.h, wbs_cons[ow])
                        lw = sr[oc % 2]
                        lw.inc(GPS.dma_start(
                            wbs[ow][:, 0:c.NFS, :],
                            wsd_t[:, oc * c.OCW:(oc + 1) * c.OCW].rearrange(
                                "(ftt p) d -> p ftt d", p=128)), 16)
                        my_n = lw.n
                        for tl in range(c.NTH):
                            tpp = tl % 2
                            bk = tl % 2
                            row0 = t0 + tl * 128
                            SYN.wait_ge(s_sc.h, xr_done)
                            SYN.wait_ge(c_ve.h, xrre_free[tpp])
                            ldx = sp[tpp]
                            ldx.inc(SYN.dma_start(
                                re_t[tpp][:, 0:c.OCW],
                                re_d[0][row0:row0 + 128, oc * c.OCW:(oc + 1) * c.OCW]),
                                16)
                            TEN.wait_ge(lw.h, my_n)
                            TEN.wait_ge(c_ve.h, max(up_done, pa_free[bk]))
                            for ftt in range(c.NFS):
                                ins = TEN.matmul(
                                    pa[bk][:, 0:c.OCW],
                                    sgu[:, ftt, tl * 128:(tl + 1) * 128],
                                    wbs[ow][:, ftt, :],
                                    start=(ftt == 0), stop=(ftt == c.NFS - 1))
                            c_mm.inc(ins)
                            wbs_cons[ow] = c_mm.n
                            c_mm.wait(VEC)
                            VEC.wait_ge(ldx.h, ldx.n)
                            VEC.wait_ge(s_out.h, out_free[tpp])
                            c_ve.inc(VEC.tensor_tensor(
                                out_t[tpp][:, 0:c.OCW], pa[bk][:, 0:c.OCW],
                                re_t[tpp][:, 0:c.OCW], ALU.add))
                            pa_free[bk] = c_ve.n
                            xrre_free[tpp] = c_ve.n
                            SYN.wait_ge(c_ve.h, c_ve.n)
                            s_out.inc(SYN.dma_start(
                                out[row0:row0 + 128, oc * c.OCW:(oc + 1) * c.OCW],
                                out_t[tpp][:, 0:c.OCW]), 16)
                            out_free[tpp] = s_out.n


        if getattr(cfg, "debug", False):
            barrier()
            for name, src in [("dbg_xn1", xn1_d), ("dbg_q", q_d), ("dbg_k", k_d),
                              ("dbg_v", v_d), ("dbg_xp", xp_d), ("dbg_xn2", xn2_d),
                              ("dbg_re", re_d[0]), ("dbg_wf", wf_d),
                              ("dbg_xn2t", xn2t_d), ("dbg_idx0", idx_d[0])]:
                dst = nc.dram_tensor(name, list(src.tensor.shape), src.dtype,
                                     kind="ExternalOutput").ap()
                s_out.inc(SYN.dma_start(dst, src), 16)

        SYN.wait_ge(s_out.h, s_out.n)
        GPS.wait_ge(s_out.h, s_out.n)
    return nc


def _host_inputs(cfg, core_id, n_chunks, inp):
    c = cfg
    b = core_id // n_chunks
    cq = core_id % n_chunks
    x = np.asarray(inp["x"])
    x_pad = np.zeros((c.T_FRAME, c.D), np.float32)
    lo = cq * c.T_OWN - c.T_EXT
    src_lo = max(0, lo)
    x_pad[src_lo - lo:, :] = x[b, src_lo:(cq + 1) * c.T_OWN, :]

    pad_rows = max(0, -lo)
    pad_tiles = pad_rows // 128
    validc1 = np.zeros((128, c.NTT), np.float32)
    validc1[:, pad_tiles:] = 1.0

    pos = np.maximum(np.arange(lo, (cq + 1) * c.T_OWN), 0).astype(np.float64)
    invf = 1.0 / THETA ** (np.arange(0, 128, 2, dtype=np.float64)[:64] / 128.0)
    freqs = pos[:, None] * invf[None, :]
    emb = np.concatenate([freqs, freqs], axis=1)
    cos_tab = np.cos(emb).astype(np.float32)
    sin_tab = np.sin(emb).astype(np.float32)
    sin_tab[:, 0:64] *= -1.0

    ii = np.arange(128)
    maskd = np.where(ii[None, :] >= ii[:, None], 0.0, NEG).astype(BFNP)
    maskw = np.where(ii[None, :] <= ii[:, None], 0.0, NEG).astype(BFNP)
    identm = np.eye(128, dtype=np.float32).astype(BFNP)
    triu_m = np.where(ii[:, None] <= ii[None, :], 1.0, 0.0).astype(BFNP)
    iota_m = (np.arange(c.T_OWN, dtype=np.int32)
              .reshape(c.NTT_OWN, 128).T.copy())

    return {
        "x_pad": x_pad,
        "wqkv": inp["w_qkv"],
        "wout": inp["w_out"],
        "wr_t": np.ascontiguousarray(np.asarray(inp["Wr"]).T),
        "wsg_t": np.ascontiguousarray(np.asarray(inp["Wsg"]).T),
        "wsu_t": np.ascontiguousarray(np.asarray(inp["Wsu"]).T),
        "wsd_t": np.ascontiguousarray(np.asarray(inp["Wsd"]).T),
        "wg_t": np.ascontiguousarray(np.asarray(inp["Wg"]).transpose(0, 2, 1)),
        "wu_t": np.ascontiguousarray(np.asarray(inp["Wu"]).transpose(0, 2, 1)),
        "wd_t": np.ascontiguousarray(np.asarray(inp["Wd"]).transpose(0, 2, 1)),
        "gq_row": np.asarray(inp["q_gamma"]).reshape(1, 128),
        "gk_row": np.asarray(inp["k_gamma"]).reshape(1, 128),
        "cos_t": cos_tab,
        "sin_t": sin_tab,
        "maskd": maskd,
        "maskw": maskw,
        "ident": identm,
        "validc1": validc1,
        "triu": triu_m,
        "iota_h": iota_m,
    }


def kernel(L0_attn_x, L0_attn_w_qkv, L0_attn_w_out, L0_attn_q_gamma, L0_attn_k_gamma,
           L0_ffn_Wr, L0_ffn_Wg, L0_ffn_Wu, L0_ffn_Wd, L0_ffn_Wsg, L0_ffn_Wsu,
           L0_ffn_Wsd):
    cfg = FULL
    B, N, D = 2, 8192, 2048
    n_chunks = N // cfg.T_OWN
    inp = {
        "x": np.asarray(L0_attn_x, np.float32),
        "w_qkv": np.asarray(L0_attn_w_qkv, np.float32),
        "w_out": np.asarray(L0_attn_w_out, np.float32),
        "q_gamma": np.asarray(L0_attn_q_gamma, np.float32),
        "k_gamma": np.asarray(L0_attn_k_gamma, np.float32),
        "Wr": np.asarray(L0_ffn_Wr, np.float32),
        "Wg": np.asarray(L0_ffn_Wg, np.float32),
        "Wu": np.asarray(L0_ffn_Wu, np.float32),
        "Wd": np.asarray(L0_ffn_Wd, np.float32),
        "Wsg": np.asarray(L0_ffn_Wsg, np.float32),
        "Wsu": np.asarray(L0_ffn_Wsu, np.float32),
        "Wsd": np.asarray(L0_ffn_Wsd, np.float32),
    }
    nc = _build(cfg)
    in_maps = [_host_inputs(cfg, cid, n_chunks, inp) for cid in range(8)]
    res = run_bass_kernel_spmd(nc, in_maps, core_ids=list(range(8)))
    outp = np.empty((B, N, D), np.float32)
    for cid in range(8):
        b, cq = cid // n_chunks, cid % n_chunks
        outp[b, cq * cfg.T_OWN:(cq + 1) * cfg.T_OWN, :] = res.results[cid]["out"]
    return outp
